# version 20
# baseline (speedup 1.0000x reference)
import numpy as np
import contextlib

import concourse.bass as bass
import concourse.mybir as mybir
from concourse.bass_utils import run_bass_kernel_spmd

F32 = mybir.dt.float32
BF16 = mybir.dt.bfloat16
AF = mybir.ActivationFunctionType
ALU = mybir.AluOpType
AX = mybir.AxisListType

N_CORES = 8
D = 1024
CC = 8
NE = 4
NIS = 8
NH = 16
HP = NH // 2
LN1_EPS = 1e-5
LN2_EPS = 1e-12
ENGS = ("sync", "scalar", "vector", "tensor", "gpsimd")


class Sem:
    def __init__(self, handle, name):
        self.h = handle
        self.name = name
        self.n = 0


class Emitter:

    def __init__(self):
        self.ops = {e: [] for e in ENGS}

    def emit(self, eng, fn, waits=(), incs=()):
        ws = tuple((s, v) for (s, v) in waits if v and v > 0)
        for s, k in incs:
            s.n += k
        self.ops[eng].append((fn, ws, tuple(incs)))

    def run(self, eng_name, eng):
        for fn, waits, incs in self.ops[eng_name]:
            for s, v in waits:
                eng.wait_ge(s.h, v)
            instr = fn(eng) if fn is not None else None
            for s, k in incs:
                instr.then_inc(s.h, k)


def pack_inputs(inputs):
    f = np.ascontiguousarray
    d = {}

    def packT(w):
        return f(w.T.reshape(CC, 128, D).transpose(1, 0, 2).reshape(128, CC * D))

    d["qwT"] = packT(np.asarray(inputs["L0_attn_q_w"], np.float32))
    d["kwT"] = packT(np.asarray(inputs["L0_attn_k_w"], np.float32))
    d["vwT"] = packT(np.asarray(inputs["L0_attn_v_w"], np.float32))
    d["owT"] = packT(np.asarray(inputs["L0_attn_out_w"], np.float32))
    d["qb"] = f(np.asarray(inputs["L0_attn_q_b"], np.float32).reshape(CC, 128).T)
    d["kb"] = f(np.asarray(inputs["L0_attn_k_b"], np.float32).reshape(CC, 128).T)
    cw = np.asarray(inputs["L0_attn_conv_w"], np.float32).reshape(D, 3)
    d["convw"] = f(cw.reshape(CC, 128, 3).transpose(1, 0, 2).reshape(128, CC * 3))
    d["convb"] = f(np.asarray(inputs["L0_attn_conv_b"], np.float32).reshape(CC, 128).T)
    wr = np.asarray(inputs["L0_ffn_W_router"], np.float32)
    d["wr"] = f(wr.reshape(CC, 128, NE).transpose(1, 0, 2).reshape(128, CC * NE))
    w1 = np.asarray(inputs["L0_ffn_W1"], np.float32)
    d["w1"] = f(w1.reshape(NE, CC, 128, NIS, 512).transpose(0, 3, 2, 1, 4)
                .reshape(NE * NIS, 128, CC * 512))
    b1 = np.asarray(inputs["L0_ffn_b1"], np.float32)
    d["b1"] = f(b1.reshape(NE, 32, 128).transpose(2, 0, 1).reshape(128, NE * 32))
    wo = np.asarray(inputs["L0_ffn_Wo"], np.float32)
    d["wo"] = f(wo.reshape(NIS, 4, 128, D).transpose(0, 2, 1, 3).reshape(NIS, 128, 4 * D))
    d["rows"] = f(np.stack([
        np.asarray(inputs["L0_ffn_ln_gamma"], np.float32),
        np.asarray(inputs["L0_ffn_ln_beta"], np.float32),
        np.asarray(inputs["L0_attn_out_b"], np.float32),
        np.asarray(inputs["L0_ffn_bo"], np.float32),
        np.asarray(inputs["L0_attn_v_b"], np.float32),
    ], axis=0))
    d["eye"] = np.eye(128, dtype=np.float32)
    return d


def build_nc(seq=2048, nb=2, debug=False):
    NT = seq // 128
    KEYS = seq // 2
    NKC = KEYS // 128
    NKB = max(1, KEYS // 512)
    NQB = seq // 512
    NTB = NQB
    XCOL = seq + 8

    nc = bass.Bass("TRN2", target_bir_lowering=False, debug=debug)

    x_d = nc.dram_tensor("x", [nb * seq, D], F32, kind="ExternalInput").ap()
    out_d = nc.dram_tensor("out", [nb * seq, D], F32, kind="ExternalOutput").ap()
    qwT_d = nc.dram_tensor("qwT", [128, CC * D], F32, kind="ExternalInput").ap()
    kwT_d = nc.dram_tensor("kwT", [128, CC * D], F32, kind="ExternalInput").ap()
    vwT_d = nc.dram_tensor("vwT", [128, CC * D], F32, kind="ExternalInput").ap()
    owT_d = nc.dram_tensor("owT", [128, CC * D], F32, kind="ExternalInput").ap()
    qb_d = nc.dram_tensor("qb", [128, CC], F32, kind="ExternalInput").ap()
    kb_d = nc.dram_tensor("kb", [128, CC], F32, kind="ExternalInput").ap()
    convw_d = nc.dram_tensor("convw", [128, CC * 3], F32, kind="ExternalInput").ap()
    convb_d = nc.dram_tensor("convb", [128, CC], F32, kind="ExternalInput").ap()
    wr_d = nc.dram_tensor("wr", [128, CC * NE], F32, kind="ExternalInput").ap()
    w1_d = nc.dram_tensor("w1", [NE * NIS, 128, CC * 512], F32, kind="ExternalInput").ap()
    b1_d = nc.dram_tensor("b1", [128, NE * 32], F32, kind="ExternalInput").ap()
    wo_d = nc.dram_tensor("wo", [NIS, 128, 4 * D], F32, kind="ExternalInput").ap()
    rows_d = nc.dram_tensor("rows", [5, D], F32, kind="ExternalInput").ap()
    scrW1 = nc.dram_tensor("scrW1", [NE * NIS, 128, CC * 512], BF16, kind="Internal").ap()
    scrWo = nc.dram_tensor("scrWo", [NIS, 128, 4 * D], BF16, kind="Internal").ap()
    eye_d = nc.dram_tensor("eye", [128, 128], F32, kind="ExternalInput").ap()

    st = contextlib.ExitStack()

    def sb(name, shape, dt):
        return st.enter_context(nc.sbuf_tensor(name, shape, dt))

    def mksem(name):
        return Sem(st.enter_context(nc.semaphore(name)), name)

    sCONST = mksem("sCONST")
    sCAST = [mksem(f"sCAST{i}") for i in range(2)]
    sSCR = [mksem(f"sSCR{i}") for i in range(2)]
    sQW = mksem("sQW")
    sX = [mksem(f"sX{i}") for i in range(2)]
    sROW = mksem("sROW")
    sOU = [mksem(f"sOU{i}") for i in range(2)]
    sW1 = [mksem(f"sW1_{i}") for i in range(4)]
    sWo = [mksem(f"sWo{i}") for i in range(2)]
    sW1g = [mksem(f"sW1g{i}") for i in range(4)]
    sWog = [mksem(f"sWog{i}") for i in range(2)]
    sT = mksem("sT")
    sSC = mksem("sSC")
    sV = mksem("sV")

    E = Emitter()

    eye_f = sb("eye_f", [128, 128], F32)
    ones_f = sb("ones_f", [1, 128], F32)
    ones_bf = sb("ones_bf", [1, 128], BF16)
    qb_sb = sb("qb_sb", [128, CC], F32)
    kb_sb = sb("kb_sb", [128, CC], F32)
    b1_sb = sb("b1_sb", [128, NE * 32], F32)
    convw_sb = sb("convw_sb", [128, CC * 3], F32)
    convb_sb = sb("convb_sb", [128, CC], F32)
    wr_f = sb("wr_f", [128, CC * NE], F32)
    vb_bf = sb("vb_bf", [1, D], BF16)
    eye_bf = sb("eye_bf", [128, 128], BF16)
    onesq_bf = sb("onesq_bf", [128, 128], BF16)
    murstd = sb("murstd", [128, 2 * nb * NT], F32)
    masks_all = sb("masks_all", [128, nb * NT * NE], F32)
    xnT = sb("xnT", [128, CC, XCOL], BF16)
    attn_sb = sb("attn_sb", [128, NT, D], BF16)
    arena = sb("arena", [128, 6 * D], F32)
    stats = sb("stats", [128, 16], F32)
    cst = sb("cst", [128, 2], F32)
    lgt = sb("lgt", [128, 8], F32)

    ar_xbuf = [arena[:, 0 * D:1 * D], arena[:, 1 * D:2 * D]]
    ar_xn = [arena[:, 2 * D:3 * D], arena[:, 3 * D:4 * D]]
    ar_fa = arena[:, 2 * D:3 * D]
    ar_fb = arena[:, 3 * D:4 * D]
    ar_out = [arena[:, 4 * D:5 * D], arena[:, 5 * D:6 * D]]

    mark = {
        "xnT_free": 0,
        "masksT_free": 0,
        "attn_free": 0,
        "arena_fin": 0,
        "ones_v": 0,
        "A_sc": 0,
        "A_v": 0,
        "wslot": 0,
    }

    def barrier(extra=()):
        t, sc, v = sT.n, sSC.n, sV.n
        for eng in ENGS:
            w = []
            if eng != "tensor":
                w.append((sT, t))
            if eng != "scalar":
                w.append((sSC, sc))
            if eng != "vector":
                w.append((sV, v))
            w += list(extra)
            E.emit(eng, None, waits=w)

    def startup():
        sctx = contextlib.ExitStack()
        for dst, src in [(eye_f, eye_d), (qb_sb, qb_d), (kb_sb, kb_d), (b1_sb, b1_d),
                         (convw_sb, convw_d), (convb_sb, convb_d), (wr_f, wr_d)]:
            E.emit("gpsimd", lambda g, d_=dst, s_=src: g.dma_start(d_[:, :], s_[:, :]),
                   incs=[(sCONST, 16)])
        E.emit("gpsimd", lambda g: g.dma_start(vb_bf[:, :], rows_d[4:5, :]), incs=[(sCONST, 16)])
        E.emit("gpsimd", lambda g: g.dma_start(eye_bf[:, :], eye_d[:, :]), incs=[(sCONST, 16)])
        ld_consts = sCONST.n
        mark["consts"] = ld_consts
        E.emit("vector", lambda v: v.memset(onesq_bf[:, :], 1.0))
        E.emit("vector", lambda v: v.memset(ones_f[:, :], 1.0))
        E.emit("vector", lambda v: v.memset(ones_bf[:, :], 1.0))
        E.emit("vector", lambda v: v.memset(xnT[:, :, 0:8], 0.0))
        E.emit("vector", lambda v: v.memset(cst[:, 0:1], LN1_EPS))
        E.emit("vector", lambda v: v.memset(cst[:, 1:2], LN2_EPS), incs=[(sV, 1)])
        mark["ones_v"] = sV.n
        sctx.close()
        return (0, 0)

    n_scr_total = startup()

    cast_ctx = contextlib.ExitStack()
    cslot = cast_ctx.enter_context(nc.sbuf_tensor("castslot", [128, 2, 1024], BF16))
    mark["scr_pieces"] = {}
    cast_state = {"chain": [(sCAST[0], 0), (sCAST[1], 0)], "idx": 0, "ops": []}
    for piece in range(NE * NIS):
        for half in range(4):
            sl = half % 2
            srcap = w1_d[piece, :, half * 1024:(half + 1) * 1024]
            dstap = scrW1[piece, :, half * 1024:(half + 1) * 1024]
            cast_state["ops"].append((piece, sl, srcap, dstap))

    def drain_cast(k):
        ops, chain = cast_state["ops"], cast_state["chain"]
        for _ in range(k):
            if cast_state["idx"] >= len(ops):
                return
            piece, sl, srcap, dstap = ops[cast_state["idx"]]
            cast_state["idx"] += 1
            E.emit("gpsimd", lambda g, sl_=sl, s_=srcap: g.dma_start(
                cslot[:, sl_, :], s_),
                waits=[chain[sl], (sCONST, mark["consts"])], incs=[(sCAST[sl], 16)])
            mld = sCAST[sl].n
            E.emit("gpsimd", lambda g, sl_=sl, d_=dstap: g.dma_start(
                d_, cslot[:, sl_, :]),
                waits=[(sCAST[sl], mld)], incs=[(sCAST[sl], 16)])
            chain[sl] = (sCAST[sl], sCAST[sl].n)
            if cast_state["idx"] % 4 == 0:
                mark["scr_pieces"][piece] = (chain[0], chain[1])


    def phaseA(b):
        barrier(extra=[(sCONST, mark["consts"])])
        ctxA = contextlib.ExitStack()
        ptr0 = ctxA.enter_context(nc.psum_tensor(f"psA_tr0_{b}", [128, 128], F32))
        ptr1 = ctxA.enter_context(nc.psum_tensor(f"psA_tr1_{b}", [128, 128], F32))
        ps_r = ctxA.enter_context(nc.psum_tensor(f"psA_r_{b}", [128, 2 * NE], F32))
        xnT32 = ctxA.enter_context(nc.sbuf_tensor(f"xnT32_{b}", [128, CC, 128], F32))
        dump_t = ctxA.enter_context(nc.sbuf_tensor(f"dumpA_{b}", [128, D], F32))
        ptr = [ptr0, ptr1]
        m_masks_prev = 0
        m_masks_par = [0, 0]
        m_evac = [0] * (2 * CC)
        NTOT = nb * NT
        xsl = [arena[:, 0:2 * D], arena[:, 4 * D:6 * D]]
        xn = arena[:, 2 * D:4 * D]
        m_xslot_free = [0, 0]
        m_stats_free = [0, 0]
        m_tr_done = 0
        NP = NT // 2
        m_lgt_prev = [0]
        m_router_prev = [0]
        m_evacS = [0] * (2 * CC)
        m_evacV = [0] * (2 * CC)
        loads = {}
        ms0s = {}
        ms2s = {}
        mxns = {}
        mlgts = {}
        trd = {}

        def emit_load(k):
            par = k % 2
            g0 = b * NT + 2 * k
            E.emit("sync", lambda s, g_=g0, p_=par: s.dma_start(
                xsl[p_].rearrange("p (a d) -> p a d", a=2),
                x_d[g_ * 128:(g_ + 2) * 128, :].rearrange("(a p) d -> p a d", p=128)),
                waits=[(sV, m_xslot_free[par]), (sV, mark["arena_fin"]),
                       (sOU[0], sOU[0].n), (sOU[1], sOU[1].n)],
                incs=[(sX[par], 16)])
            loads[k] = sX[par].n

        def emit_accums(k):
            par = k % 2
            so = 8 * par
            xb = xsl[par]
            for a in range(2):
                w0 = ([(sX[par], loads[k]), (sV, m_stats_free[par]),
                       (sV, mark["arena_fin"] if k < 2 else 0),
                       (sT, mark["xnT_free"] if k < 2 else 0)]
                      if a == 0 else [(sSC, sSC.n)])
                E.emit("scalar", lambda s, xb_=xb, a_=a, so_=so: s.activation(
                    dump_t[:, :], xb_[:, a_ * D:(a_ + 1) * D], AF.Identity,
                    accum_out=stats[:, so_ + a_:so_ + a_ + 1]),
                    waits=w0, incs=[(sSC, 1)])
                E.emit("scalar", lambda s, xb_=xb, a_=a, so_=so: s.activation(
                    dump_t[:, :], xb_[:, a_ * D:(a_ + 1) * D], AF.Square,
                    accum_out=stats[:, so_ + 2 + a_:so_ + 3 + a_]),
                    waits=[(sSC, sSC.n)], incs=[(sSC, 1)])
            ms0s[k] = sSC.n

        def emit_smalls_lnexp(k):
            par = k % 2
            so = 8 * par
            g0 = b * NT + 2 * k
            E.emit("vector", lambda v, so_=so, g_=g0: v.tensor_scalar(
                murstd[:, g_:g_ + 2], stats[:, so_:so_ + 2], 1.0 / D, None, op0=ALU.mult),
                waits=[(sSC, ms0s[k])], incs=[(sV, 1)])
            mv4 = sV.n
            E.emit("vector", lambda v, so_=so: v.tensor_scalar(
                stats[:, so_ + 4:so_ + 6], stats[:, so_ + 2:so_ + 4], 1.0 / D, None,
                op0=ALU.mult),
                waits=[(sSC, ms0s[k])], incs=[(sV, 1)])
            m_stats_free[par] = sV.n
            E.emit("vector", lambda v, so_=so, g_=g0: v.tensor_tensor(
                stats[:, so_ + 6:so_ + 8], murstd[:, g_:g_ + 2], murstd[:, g_:g_ + 2],
                op=ALU.mult),
                waits=[(sV, mv4)], incs=[(sV, 1)])
            mv6 = sV.n
            E.emit("vector", lambda v, so_=so: v.tensor_tensor(
                stats[:, so_ + 4:so_ + 6], stats[:, so_ + 4:so_ + 6],
                stats[:, so_ + 6:so_ + 8], op=ALU.subtract),
                waits=[(sV, mv6)], incs=[(sV, 1)])
            mv7 = sV.n
            E.emit("scalar", lambda s, so_=so: s.activation(
                stats[:, so_ + 6:so_ + 8], stats[:, so_ + 4:so_ + 6], AF.Ln, bias=cst[:, 0:1]),
                waits=[(sV, mv7), (sV, m_masks_par[par])], incs=[(sSC, 1)])
            ms1 = sSC.n
            E.emit("scalar", lambda s, so_=so, g_=g0: s.activation(
                murstd[:, NTOT + g_:NTOT + g_ + 2], stats[:, so_ + 6:so_ + 8],
                AF.Exp, scale=-0.5),
                waits=[(sSC, ms1)], incs=[(sSC, 1)])
            ms2s[k] = sSC.n

        def emit_xn(k):
            par = k % 2
            g0 = b * NT + 2 * k
            xb = xsl[par]
            for a in range(2):
                E.emit("vector", lambda v, xb_=xb, a_=a, g_=g0 + a: v.tensor_scalar(
                    xn[:, a_ * D:(a_ + 1) * D], xb_[:, a_ * D:(a_ + 1) * D],
                    murstd[:, g_:g_ + 1], murstd[:, NTOT + g_:NTOT + g_ + 1],
                    op0=ALU.subtract, op1=ALU.mult),
                    waits=[(sSC, ms2s[k]) if a == 0 else (sV, 0),
                           (sT, trd.get(k - 1, 0)) if a == 0 else (sV, 0),
                           (sV, mark["arena_fin"] if k < 2 else 0)],
                    incs=[(sV, 1)])
            mxns[k] = sV.n
            m_xslot_free[par] = mxns[k]

        def emit_tr(k):
            t0 = 2 * k
            m_router_prev2 = 0
            last_tr = 0
            for a in range(2):
                t = t0 + a
                for cc in range(CC):
                    ci = a * CC + cc
                    tp = ci % 2
                    E.emit("tensor", lambda t_, cc_=cc, tp_=tp, a_=a: t_.transpose(
                        ptr[tp_][:, :], xn[:, a_ * D + cc_ * 128:a_ * D + (cc_ + 1) * 128],
                        eye_f[:, :]),
                        waits=[(sV, mxns[k]), (sV, m_evacV[ci - 2]),
                               (sT, mark["xnT_free"] if (k == 0 and ci == 0) else 0)],
                        incs=[(sT, 1)])
                    mtr = sT.n
                    last_tr = mtr
                    E.emit("vector", lambda v, cc_=cc, tp_=tp: v.tensor_copy(
                        xnT32[:, cc_, :], ptr[tp_][:, :]),
                        waits=[(sT, mtr), (sT, m_router_prev2 or m_router_prev[0]),
                               (sSC, m_evacS[(ci + 8) % 16])],
                        incs=[(sV, 1)])
                    m_evacV[ci] = sV.n
                    E.emit("scalar", lambda s, cc_=cc, t_=t: s.copy(
                        xnT[:, cc_, 8 + t_ * 128:8 + (t_ + 1) * 128], xnT32[:, cc_, :]),
                        waits=[(sV, m_evacV[ci]),
                               (sT, mark["xnT_free"] if (k == 0 and ci == 0) else 0)],
                        incs=[(sSC, 1)])
                    m_evacS[ci] = sSC.n
                for cc in range(CC):
                    ci = a * CC + cc
                    w = [(sV, m_evacV[ci])]
                    if cc == 0:
                        w.append((sV, m_lgt_prev[0]))
                    E.emit("tensor", lambda t_, cc_=cc, a_=a, f_=(cc == 0), l_=(cc == CC - 1):
                           t_.matmul(ps_r[:, 4 * a_:4 * a_ + 4], xnT32[:, cc_, :],
                                     wr_f[:, cc_ * NE:(cc_ + 1) * NE], start=f_, stop=l_),
                           waits=w, incs=[(sT, 1)] if cc == CC - 1 else ())
                if a == 0:
                    m_router_prev2 = sT.n
            m_router_prev[0] = sT.n
            trd[k] = last_tr
            E.emit("vector", lambda v: v.tensor_copy(lgt[:, 0:8], ps_r[:, :]),
                   waits=[(sT, m_router_prev[0])], incs=[(sV, 1)])
            mlgts[k] = sV.n
            m_lgt_prev[0] = sV.n

        def emit_masks(k):
            nonlocal m_masks_prev
            par = k % 2
            so = 8 * par
            g0 = b * NT + 2 * k
            mo = NE * g0
            mprev = 0
            for e in range(NE):
                js = [j for j in range(NE) if j != e]
                ops = [ALU.is_gt if j < e else ALU.is_ge for j in js]
                w0 = [(sV, mlgts[k]), (sV, mprev), (sSC, ms2s[k])]
                if k == 0 and e == 0:
                    w0.append((sT, mark["masksT_free"]))
                E.emit("vector", lambda v, e_=e, mo_=mo, j_=js[0], o_=ops[0]: v.tensor_tensor(
                    masks_all[:, mo_ + e_:mo_ + e_ + 5:4], lgt[:, e_:e_ + 5:4],
                    lgt[:, j_:j_ + 5:4], op=o_),
                    waits=w0)
                E.emit("vector", lambda v, e_=e, so_=so, j_=js[1], o_=ops[1]: v.tensor_tensor(
                    stats[:, so_ + 4:so_ + 6], lgt[:, e_:e_ + 5:4],
                    lgt[:, j_:j_ + 5:4], op=o_))
                E.emit("vector", lambda v, e_=e, so_=so, j_=js[2], o_=ops[2]: v.tensor_tensor(
                    stats[:, so_ + 6:so_ + 8], lgt[:, e_:e_ + 5:4],
                    lgt[:, j_:j_ + 5:4], op=o_),
                    incs=[(sV, 1)])
                mc = sV.n
                E.emit("vector", lambda v, so_=so: v.tensor_tensor(
                    stats[:, so_ + 4:so_ + 6], stats[:, so_ + 4:so_ + 6],
                    stats[:, so_ + 6:so_ + 8], op=ALU.mult),
                    waits=[(sV, mc)], incs=[(sV, 1)])
                mc2 = sV.n
                E.emit("vector", lambda v, e_=e, mo_=mo, so_=so: v.tensor_tensor(
                    masks_all[:, mo_ + e_:mo_ + e_ + 5:4], masks_all[:, mo_ + e_:mo_ + e_ + 5:4],
                    stats[:, so_ + 4:so_ + 6], op=ALU.mult),
                    waits=[(sV, mc2)], incs=[(sV, 1)])
                mprev = sV.n
            m_masks_prev = sV.n
            m_masks_par[par] = sV.n

        for kpair in range(NP):
            if kpair == 0:
                emit_load(0)
                emit_accums(0)
            emit_smalls_lnexp(kpair)
            if kpair + 1 < NP:
                emit_load(kpair + 1)
                emit_accums(kpair + 1)
            emit_xn(kpair)
            if kpair >= 1:
                emit_masks(kpair - 1)
            emit_tr(kpair)
            if b == 0:
                drain_cast(6)
        emit_masks(NP - 1)
        ctxA.close()
        mark["A_sc"] = sSC.n
        mark["A_v"] = sV.n

    def phaseC(b, n_scr_total):
        barrier()
        ctx = contextlib.ExitStack()
        qT = ctx.enter_context(nc.sbuf_tensor(f"qT{b}", [128, CC, seq], BF16))
        kT = ctx.enter_context(nc.sbuf_tensor(f"kT{b}", [128, CC, KEYS], BF16))
        v_sb = ctx.enter_context(nc.sbuf_tensor(f"v{b}", [128, NKC, NH, 65], BF16))
        wslot = ctx.enter_context(nc.sbuf_tensor(f"wslot{b}", [128, CC * D], BF16))
        ctx2 = contextlib.ExitStack()
        xdT = ctx2.enter_context(nc.sbuf_tensor(f"xdT{b}", [128, CC, KEYS], BF16))
        m_A = mark["A_sc"]
        dc = drain_cast if b == 0 else (lambda k: None)

        def load_w(src):
            E.emit("gpsimd", lambda g_, s_=src: g_.dma_start(wslot[:, :], s_[:, :]),
                   waits=[(sT, mark["wslot"])], incs=[(sQW, 16)])
            return sQW.n

        m_conv = 0
        for cc in range(CC):
            t1 = xnT[:, cc, 7:7 + 2 * (KEYS - 1) + 1:2]
            t2 = xnT[:, cc, 8:8 + 2 * (KEYS - 1) + 1:2]
            t3 = xnT[:, cc, 9:9 + 2 * (KEYS - 1) + 1:2]
            E.emit("vector", lambda v, t_=t1, cc_=cc: v.tensor_scalar(
                ar_fa[:, 0:KEYS], t_, convw_sb[:, 3 * cc_:3 * cc_ + 1], None, op0=ALU.mult),
                waits=[(sSC, m_A), (sV, mark["arena_fin"]), (sV, m_conv)],
                incs=[(sV, 1)])
            mvc = sV.n
            E.emit("vector", lambda v, t_=t2, cc_=cc: v.scalar_tensor_tensor(
                ar_fb[:, 0:KEYS], t_, convw_sb[:, 3 * cc_ + 1:3 * cc_ + 2], ar_fa[:, 0:KEYS],
                op0=ALU.mult, op1=ALU.add),
                waits=[(sV, mvc)], incs=[(sV, 1)])
            mvc = sV.n
            E.emit("vector", lambda v, t_=t3, cc_=cc: v.scalar_tensor_tensor(
                ar_fa[:, 0:KEYS], t_, convw_sb[:, 3 * cc_ + 2:3 * cc_ + 3], ar_fb[:, 0:KEYS],
                op0=ALU.mult, op1=ALU.add),
                waits=[(sV, mvc)], incs=[(sV, 1)])
            mvc = sV.n
            E.emit("vector", lambda v, cc_=cc: v.tensor_scalar(
                xdT[:, cc_, :], ar_fa[:, 0:KEYS], convb_sb[:, cc_:cc_ + 1], None, op0=ALU.add),
                waits=[(sV, mvc)], incs=[(sV, 1)])
            m_conv = sV.n
        m_conv_all = m_conv

        with (nc.psum_tensor(f"psC0{b}", [128, 512], F32) as pc0,
              nc.psum_tensor(f"psC1{b}", [128, 512], F32) as pc1):
            pc = [pc0, pc1]
            m_ev = [0, 0]
            kmm = 0
            mq = load_w(qwT_d)
            for oc in range(CC):
                for tb in range(NQB):
                    pp = kmm % 2
                    kmm += 1
                    for cc in range(CC):
                        w = [(sQW, mq), (sSC, m_A), (sSC, m_ev[pp])] if cc == 0 else []
                        E.emit("tensor", lambda t_, oc_=oc, tb_=tb, cc_=cc, pp_=pp: t_.matmul(
                            pc[pp_][:, :],
                            wslot[:, cc_ * D + oc_ * 128:cc_ * D + (oc_ + 1) * 128],
                            xnT[:, cc_, 8 + tb_ * 512:8 + (tb_ + 1) * 512],
                            start=(cc_ == 0), stop=(cc_ == CC - 1)),
                            waits=w, incs=[(sT, 1)] if cc == CC - 1 else ())
                    mmm = sT.n
                    E.emit("scalar", lambda s, oc_=oc, tb_=tb, pp_=pp: s.activation(
                        qT[:, oc_, tb_ * 512:(tb_ + 1) * 512], pc[pp_][:, :], AF.Identity,
                        bias=qb_sb[:, oc_:oc_ + 1]),
                        waits=[(sT, mmm)], incs=[(sSC, 1)])
                    m_ev[pp] = sSC.n
            mark["wslot"] = sT.n
            mk_ = load_w(kwT_d)
            for oc in range(CC):
                for kb in range(NKB):
                    klen = min(512, KEYS - kb * 512)
                    pp = kmm % 2
                    kmm += 1
                    for cc in range(CC):
                        w = [(sQW, mk_), (sV, m_conv_all), (sSC, m_ev[pp])] if cc == 0 else []
                        E.emit("tensor", lambda t_, oc_=oc, kb_=kb, cc_=cc, pp_=pp, kl=klen:
                               t_.matmul(pc[pp_][:, 0:kl],
                                         wslot[:, cc_ * D + oc_ * 128:cc_ * D + (oc_ + 1) * 128],
                                         xdT[:, cc_, kb_ * 512:kb_ * 512 + kl],
                                         start=(cc_ == 0), stop=(cc_ == CC - 1)),
                               waits=w, incs=[(sT, 1)] if cc == CC - 1 else ())
                    mmm = sT.n
                    E.emit("scalar", lambda s, oc_=oc, kb_=kb, pp_=pp, kl=klen: s.activation(
                        kT[:, oc_, kb_ * 512:kb_ * 512 + kl], pc[pp_][:, 0:kl], AF.Identity,
                        bias=kb_sb[:, oc_:oc_ + 1]),
                        waits=[(sT, mmm)], incs=[(sSC, 1)])
                    m_ev[pp] = sSC.n
            mark["wslot"] = sT.n
            mv_ = load_w(vwT_d)
            E.emit("vector", lambda v: v.memset(v_sb[:, :, :, 64:65], 1.0),
                   waits=[(sV, mark["attn_free"])], incs=[(sV, 1)])
            m_vones = sV.n
            m_vev = [0, 0]
            for kt in range(NKC):
                for oh in range(2):
                    pp = kmm % 2
                    kmm += 1
                    for cc in range(CC):
                        w = ([(sQW, mv_), (sV, m_conv_all), (sSC, m_ev[pp]),
                              (sV, m_vev[pp])] if cc == 0 else [])
                        E.emit("tensor", lambda t_, kt_=kt, oh_=oh, cc_=cc, pp_=pp: t_.matmul(
                            pc[pp_][:, :], xdT[:, cc_, kt_ * 128:(kt_ + 1) * 128],
                            wslot[:, cc_ * D + oh_ * 512:cc_ * D + (oh_ + 1) * 512],
                            start=(cc_ == 0), stop=False),
                            waits=w)
                    E.emit("tensor", lambda t_, oh_=oh, pp_=pp: t_.matmul(
                        pc[pp_][:, :], ones_bf[0:1, :], vb_bf[0:1, oh_ * 512:(oh_ + 1) * 512],
                        start=False, stop=True),
                        incs=[(sT, 1)])
                    mmm = sT.n
                    E.emit("vector", lambda v, kt_=kt, oh_=oh, pp_=pp: v.tensor_copy(
                        v_sb[:, kt_, 8 * oh_:8 * (oh_ + 1), 0:64],
                        pc[pp_][:, :].rearrange("p (h c) -> p h c", c=64)),
                        waits=[(sT, mmm), (sV, m_vones)], incs=[(sV, 1)])
                    m_vev[pp] = sV.n
            mark["wslot"] = sT.n
            m_v_done = sV.n
            m_kqv_sc = sSC.n
        ctx2.close()
        otp = ctx.enter_context(nc.sbuf_tensor(f"otp{b}", [128, HP, 512], BF16))
        rrow_t = ctx.enter_context(nc.sbuf_tensor(f"rrow{b}", [1, 512], F32))
        rcbc_t = ctx.enter_context(nc.sbuf_tensor(f"rcbc{b}", [64, 512], F32))
        rrows = [rrow_t[0:1, :], ar_fa[0:1, 0:512]]
        rcbc = rcbc_t[:, :]
        etb_t = ctx.enter_context(nc.sbuf_tensor(f"etb{b}", [128, 2 * NKC * 512], BF16))
        mo_ = load_w(owT_d)
        NOT = 3
        with (nc.psum_tensor(f"psS0{b}", [128, 1024], F32) as ps_s0,
              nc.psum_tensor(f"psS1{b}", [128, 1024], F32) as ps_s1,
              nc.psum_tensor(f"psOT0{b}", [128, 512], F32) as ps_ot0,
              nc.psum_tensor(f"psOT1{b}", [128, 512], F32) as ps_ot1,
              nc.psum_tensor(f"psOT2{b}", [128, 512], F32) as ps_ot2,
              nc.psum_tensor(f"psBC{b}", [128, 512], F32) as ps_bc):
            ps_s = [ps_s0, ps_s1]
            ring = [ps_ot0, ps_ot1, ps_ot2]
            m_ring = [None, None, None]
            rslot = [0]
            m_exp = {}
            m_ot = {}
            m_otevac = {}
            m_rc = 0
            m_rcbc = 0
            m_atp_ev = 0
            m_otp_free = 0

            def take_slot():
                r = rslot[0] % NOT
                rslot[0] += 1
                return r

            def prev_h(qb, h, k):
                hh = qb * NH + h - k
                return (hh // NH, hh % NH) if hh >= 0 else None

            def emit_ot_pair(pend, pair):
                (pqb, ph_), pr = pend
                for kc in (2 * pair, 2 * pair + 1):
                    w = [(sSC, m_exp[((pqb, ph_), pair)])]
                    if kc == 0 and m_ring[pr] is not None:
                        w.append(m_ring[pr])
                    if kc == 0 and (pqb, ph_) == (0, 0):
                        w.append((sV, m_v_done))
                    eo = (ph_ % 2) * NKC * 512
                    E.emit("tensor", lambda t_, h_=ph_, kc_=kc, r_=pr, eo_=eo, q_=pqb:
                           t_.matmul(ring[r_][0:65, :], v_sb[:, kc_, h_, :],
                                     etb_t[:, eo_ + kc_ * 512:eo_ + (kc_ + 1) * 512],
                                     start=(kc_ == 0), stop=(kc_ == NKC - 1)),
                           waits=w, incs=[(sT, 1)] if kc == NKC - 1 else ())
                if pair == NKC // 2 - 1:
                    m_ot[(pqb, ph_)] = sT.n

            m_recip = {}
            m_bcpar = [0, 0]

            def sigma_recip(pend):
                (pqb, ph_), pr = pend
                sl = (pqb * NH + ph_) % 2
                E.emit("vector", lambda v, r_=pr, sl_=sl: v.reciprocal(
                    rrows[sl_], ring[r_][64:65, :]),
                    waits=[(sT, m_ot[(pqb, ph_)]), (sT, m_bcpar[sl])], incs=[(sV, 1)])
                m_recip[(pqb, ph_)] = sV.n

            def sigma_rest(pend):
                nonlocal m_rcbc, m_otp_free
                (pqb, ph_), pr = pend
                sl = (pqb * NH + ph_) % 2
                hp2, hh2 = ph_ // 2, ph_ % 2
                E.emit("tensor", lambda t_, sl_=sl: t_.matmul(
                    ps_bc[0:64, :], ones_f[0:1, 0:64], rrows[sl_],
                    start=True, stop=True),
                    waits=[(sV, m_recip[(pqb, ph_)]), (sV, m_rcbc)], incs=[(sT, 1)])
                mbc = sT.n
                m_bcpar[sl] = mbc
                E.emit("vector", lambda v: v.tensor_copy(rcbc, ps_bc[0:64, :]),
                       waits=[(sT, mbc), (sV, m_rcbc)], incs=[(sV, 1)])
                mrcp = sV.n
                E.emit("vector", lambda v, a=hp2, b2=hh2, r_=pr: v.tensor_tensor(
                    otp[64 * b2:64 * (b2 + 1), a, :], ring[r_][0:64, :], rcbc,
                    op=ALU.mult),
                    waits=[(sV, mrcp), (sT, m_otp_free)], incs=[(sV, 1)])
                m_otevac[(pqb, ph_)] = sV.n
                m_ring[pr] = (sV, sV.n)
                m_rcbc = sV.n

            def emit_outproj(pqb):
                nonlocal m_atp_ev, m_otp_free
                m_all_otp = m_otevac[(pqb, NH - 1)]
                for tt in range(4):
                    gt = pqb * 4 + tt
                    r1, r2 = take_slot(), take_slot()
                    for oh, rr in ((0, r1), (1, r2)):
                        for oc in range(CC):
                            w = []
                            if oc == 0:
                                w = [(sV, m_all_otp), (sQW, mo_)]
                                if m_ring[rr] is not None:
                                    w.append(m_ring[rr])
                            E.emit("tensor", lambda t_, oc_=oc, oh_=oh, tt_=tt, r_=rr: t_.matmul(
                                ring[r_][:, :],
                                otp[:, oc_, tt_ * 128:(tt_ + 1) * 128],
                                wslot[:, oc_ * D + oh_ * 512:oc_ * D + (oh_ + 1) * 512],
                                start=(oc_ == 0), stop=(oc_ == CC - 1)),
                                waits=w,
                                incs=[(sT, 1)] if oc == CC - 1 else ())
                    mat = sT.n
                    for oh, rr in ((0, r1), (1, r2)):
                        E.emit("scalar", lambda s, gt_=gt, oh_=oh, r_=rr: s.copy(
                            attn_sb[:, gt_, oh_ * 512:(oh_ + 1) * 512], ring[r_][:, :]),
                            waits=[(sT, mat), (sV, mark["attn_free"])], incs=[(sSC, 1)])
                        m_ring[rr] = (sSC, sSC.n)
                    m_atp_ev = sSC.n
                m_otp_free = sT.n

            pend = None
            pend2 = None
            npair = [0]
            pair_exp = {}
            for qb in range(NQB):
                for h in range(NH):
                    hp_, hh = h // 2, h % 2
                    eoff = (h % 2) * NKC * 512
                    for pair in range(NKC // 2):
                        if pend is not None:
                            emit_ot_pair(pend, pair)
                        gp = npair[0]
                        npair[0] += 1
                        spar = gp % 2
                        for j in range(2):
                            w = []
                            if j == 0:
                                if gp >= 2:
                                    w.append((sSC, pair_exp[gp - 2]))
                                else:
                                    w.append((sSC, m_kqv_sc))
                                    w.append((sV, m_v_done))
                            E.emit("tensor", lambda t_, hp2=hp_, hh2=hh, p_=pair, qb_=qb, par_=spar, j_=j:
                                   t_.matmul(ps_s[par_][:, j_ * 512:(j_ + 1) * 512],
                                             kT[64 * hh2:64 * (hh2 + 1), hp2,
                                                (2 * p_ + j_) * 128:(2 * p_ + j_ + 1) * 128],
                                             qT[64 * hh2:64 * (hh2 + 1), hp2, qb_ * 512:(qb_ + 1) * 512],
                                             start=True, stop=True),
                                   waits=w, incs=[(sT, 1)])
                        msp = sT.n
                        ph = prev_h(qb, h, 2)
                        we = [(sT, msp), (sT, m_ot[ph] if ph else 0)]
                        E.emit("scalar", lambda s, par_=spar, pair_=pair, eo=eoff: s.activation(
                            etb_t[:, eo + pair_ * 1024:eo + (pair_ + 1) * 1024], ps_s[par_][:, :],
                            AF.Exp, scale=0.125),
                            waits=we, incs=[(sSC, 1)])
                        m_exp[((qb, h), pair)] = sSC.n
                        pair_exp[gp] = sSC.n
                    if pend is not None:
                        sigma_recip(pend)
                    if pend2 is not None:
                        sigma_rest(pend2)
                        if pend2[0][1] == NH - 1:
                            emit_outproj(pend2[0][0])
                    pend2 = pend
                    pend = ((qb, h), take_slot())
                    dc(2)
            for pair in range(NKC // 2):
                emit_ot_pair(pend, pair)
            sigma_recip(pend)
            sigma_rest(pend2)
            sigma_rest(pend)
            emit_outproj(NQB - 1)
            dc(100000)
            mark["wslot"] = sT.n
        ctx.close()

    NW1SLOT = 4

    def phaseB(b, n_scr):
        barrier(extra=[(sCAST[0], sCAST[0].n), (sCAST[1], sCAST[1].n)])
        ctx = contextlib.ExitStack()
        w1r = ctx.enter_context(nc.sbuf_tensor(f"w1r{b}", [128, NW1SLOT, CC * 512], BF16))
        wor = ctx.enter_context(nc.sbuf_tensor(f"wor{b}", [128, 2, 4 * D], BF16))
        interT = ctx.enter_context(nc.sbuf_tensor(f"interT{b}", [128, 32, 512], BF16))
        gbuf = ctx.enter_context(nc.sbuf_tensor(f"gbuf{b}", [128, 2, NE, 512], BF16))
        selb = ctx.enter_context(nc.sbuf_tensor(f"sel{b}", [128, 2, 512], BF16))
        mbc = ctx.enter_context(nc.sbuf_tensor(f"mbc{b}", [128, NE, 512], BF16))
        diag_bf = ctx.enter_context(nc.sbuf_tensor(f"diag{b}", [128, 2, 128], BF16))
        rows_slot = ctx.enter_context(nc.sbuf_tensor(f"rowsl{b}", [1, 512], F32))
        gam_bc = ctx.enter_context(nc.sbuf_tensor(f"gam_bc{b}", [128, D], F32))
        bout_bc = ctx.enter_context(nc.sbuf_tensor(f"bout_bc{b}", [128, D], F32))
        bo_bc = ctx.enter_context(nc.sbuf_tensor(f"bo_bc{b}", [128, D], F32))
        ps8 = ctx.enter_context(nc.psum_tensor(f"ps8_{b}", [128, 4096], F32))

        m_gelu = {}
        m_sel = {}
        m_fin1 = {}
        m_fin = {}
        m_yis = {}
        m_w1slot = {}
        m_w1ld = {}
        m_wosl = {}
        m_wold = {}
        m_mbc_ev = 0
        m_scfin = [0]
        m_wb = mark.setdefault("wb", {})
        m_y_done_prev = 0
        m_xfree = {}
        m_fin_prev = [0]
        m_yeread = [(sV, 0), (sV, 0)]
        fin_pend = [None]

        def drain_fin(k):
            gen = fin_pend[0]
            if gen is None:
                return
            for _ in range(k):
                if next(gen, "DONE") == "DONE":
                    fin_pend[0] = None
                    return

        def fin_emit(tb, m_y_done):
            for tt in (0, 3, 1, 2):
                t_loc = tb * 4 + tt
                g = b * NT + t_loc
                p = tt % 2
                E.emit("sync", lambda s, g_=g, p_=p: s.dma_start(
                    ar_xbuf[p_], x_d[g_ * 128:(g_ + 1) * 128, :]),
                    waits=[(sV, m_xfree.get(p, 0))],
                    incs=[(sX[p], 16)])
                mx = sX[p].n
                so = 8 * p
                if tt in (0, 3):
                    ysrc = ar_out[0 if tt == 0 else 1]
                    w1_ = [m_fin1[(tb, tt)], (sV, m_fin_prev[0])]
                else:
                    ysrc = ps8[:, tt * 1024:(tt + 1) * 1024]
                    w1_ = [(sT, m_y_done), (sV, m_fin_prev[0])]
                E.emit("vector", lambda v, ys=ysrc: v.tensor_tensor(
                    ar_fb, ys, bo_bc[:, :], op=ALU.add),
                    waits=w1_, incs=[(sV, 1)])
                if tt in (1, 2):
                    m_fin1[(tb, tt)] = (sV, sV.n)
                else:
                    m_yeread[0 if tt == 0 else 1] = (sV, sV.n)
                yield
                E.emit("vector", lambda v, p_=p, g_=g: v.tensor_scalar(
                    ar_fa, ar_xbuf[p_], murstd[:, g_:g_ + 1],
                    murstd[:, nb * NT + g_:nb * NT + g_ + 1],
                    op0=ALU.subtract, op1=ALU.mult),
                    waits=[(sX[p], mx)], incs=[(sV, 1)])
                mv = sV.n
                yield
                E.emit("vector", lambda v: v.tensor_tensor(ar_fa, ar_fb, ar_fa, op=ALU.add),
                       waits=[(sV, mv)], incs=[(sV, 1)])
                mv = sV.n
                yield
                E.emit("vector", lambda v, so_=so: v.tensor_reduce(
                    stats[:, so_ + 0:so_ + 1], ar_fa, axis=AX.X, op=ALU.add),
                    waits=[(sV, mv), (sSC, m_scfin[0])], incs=[(sV, 1)])
                mv1 = sV.n
                yield
                E.emit("vector", lambda v: v.tensor_tensor(ar_fb, ar_fa, ar_fa, op=ALU.mult),
                       waits=[(sV, mv)], incs=[(sV, 1)])
                mv2 = sV.n
                yield
                E.emit("vector", lambda v, so_=so: v.tensor_reduce(
                    stats[:, so_ + 1:so_ + 2], ar_fb, axis=AX.X, op=ALU.add),
                    waits=[(sV, mv2)], incs=[(sV, 1)])
                mv3 = sV.n
                yield
                E.emit("vector", lambda v, so_=so: v.tensor_scalar(
                    stats[:, so_ + 2:so_ + 3], stats[:, so_ + 0:so_ + 1], 1.0 / D, None,
                    op0=ALU.mult),
                    waits=[(sV, mv1)], incs=[(sV, 1)])
                mv4 = sV.n
                E.emit("vector", lambda v, so_=so: v.tensor_scalar(
                    stats[:, so_ + 3:so_ + 4], stats[:, so_ + 1:so_ + 2], 1.0 / D, None,
                    op0=ALU.mult),
                    waits=[(sV, mv3)], incs=[(sV, 1)])
                mv5 = sV.n
                yield
                E.emit("vector", lambda v, so_=so: v.tensor_tensor(
                    stats[:, so_ + 4:so_ + 5], stats[:, so_ + 2:so_ + 3],
                    stats[:, so_ + 2:so_ + 3], op=ALU.mult),
                    waits=[(sV, mv4)], incs=[(sV, 1)])
                mv6 = sV.n
                E.emit("vector", lambda v, so_=so: v.tensor_tensor(
                    stats[:, so_ + 5:so_ + 6], stats[:, so_ + 3:so_ + 4],
                    stats[:, so_ + 4:so_ + 5], op=ALU.subtract),
                    waits=[(sV, mv5), (sV, mv6)], incs=[(sV, 1)])
                mv7 = sV.n
                yield
                E.emit("scalar", lambda s, so_=so: s.activation(
                    stats[:, so_ + 6:so_ + 7], stats[:, so_ + 5:so_ + 6], AF.Ln, bias=cst[:, 1:2]),
                    waits=[(sV, mv7)], incs=[(sSC, 1)])
                msc = sSC.n
                E.emit("scalar", lambda s, so_=so: s.activation(
                    stats[:, so_ + 7:so_ + 8], stats[:, so_ + 6:so_ + 7], AF.Exp, scale=-0.5),
                    waits=[(sSC, msc)], incs=[(sSC, 1)])
                m_scfin[0] = sSC.n
                yield
                E.emit("vector", lambda v, so_=so: v.tensor_scalar(
                    ar_fb, ar_fa, stats[:, so_ + 2:so_ + 3], stats[:, so_ + 7:so_ + 8],
                    op0=ALU.subtract, op1=ALU.mult),
                    waits=[(sSC, m_scfin[0])], incs=[(sV, 1)])
                mv = sV.n
                yield
                E.emit("vector", lambda v: v.tensor_tensor(ar_fa, ar_fb, gam_bc[:, :], op=ALU.mult),
                       waits=[(sV, mv)], incs=[(sV, 1)])
                mv = sV.n
                yield
                E.emit("vector", lambda v: v.tensor_tensor(ar_fb, ar_fa, bout_bc[:, :], op=ALU.add),
                       waits=[(sV, mv)], incs=[(sV, 1)])
                mv = sV.n
                yield
                E.emit("vector", lambda v, tl=t_loc: v.tensor_tensor(
                    ar_fa, ar_fb, attn_sb[:, tl, :], op=ALU.add),
                    waits=[(sV, mv)], incs=[(sV, 1)])
                mv = sV.n
                yield
                E.emit("vector", lambda v, p_=p: v.tensor_tensor(
                    ar_out[p_], ar_fa, ar_xbuf[p_], op=ALU.add),
                    waits=[(sV, mv), (sOU[p], sOU[p].n)],
                    incs=[(sV, 1)])
                m_fin[(tb, tt)] = sV.n
                m_xfree[p] = sV.n
                m_fin_prev[0] = sV.n
                E.emit("sync", lambda s, g_=g, p_=p: s.dma_start(
                    out_d[g_ * 128:(g_ + 1) * 128, :], ar_out[p_]),
                    waits=[(sV, m_fin[(tb, tt)])], incs=[(sOU[p], 16)])
                yield

        m_scr = mark["scr_pieces"]

        def emit_w1_load(tb, ii):
            piece = (ii % NE) * NIS + (ii // NE)
            slot = ii % NW1SLOT
            w = [(sT, m_w1slot.get(slot, 0))] + list(m_scr[piece])
            E.emit("sync", lambda s, p_=piece, sl_=slot: s.dma_start(
                w1r[:, sl_, :], scrW1[p_, :, :]),
                waits=w, incs=[(sW1[slot], 16)])
            m_w1ld[(tb, ii)] = (sW1[slot], sW1[slot].n)

        def emit_wo_load(tb, is_):
            slot = is_ % 2
            w = [(sT, m_wosl.get(slot, 0))]
            E.emit("gpsimd", lambda g, p_=is_, sl_=slot: g.dma_start(
                wor[:, sl_, :], wo_d[p_, :, :]),
                waits=w, incs=[(sWog[slot], 16)])
            m_wold[(tb, is_)] = (sWog[slot], sWog[slot].n)

        for tb in range(NTB):
            for ii in range(NW1SLOT):
                emit_w1_load(tb, ii)
            for is_ in range(2):
                emit_wo_load(tb, is_)
            if tb == 0:
                mrl = 0
                for dst, srcs in [(gam_bc, [0]), (bout_bc, [1, 2]), (bo_bc, [3])]:
                    for half in range(2):
                        for i, r in enumerate(srcs):
                            E.emit("sync", lambda s, r_=r, h=half: s.dma_start(
                                rows_slot[0:1, :],
                                rows_d[r_:r_ + 1, h * 512:(h + 1) * 512]),
                                waits=[(sT, mrl)], incs=[(sROW, 16)])
                            mrow = sROW.n
                            E.emit("tensor", lambda t, f_=(i == 0), l_=(i == len(srcs) - 1):
                                   t.matmul(ps8[:, 3584:4096], ones_f[0:1, :],
                                            rows_slot[0:1, :],
                                            start=f_, stop=l_),
                                   waits=[(sROW, mrow), (sV, mark["ones_v"]),
                                          (sV, m_mbc_ev)],
                                   incs=[(sT, 1)])
                            mrl = sT.n
                        mtb = sT.n
                        E.emit("vector", lambda v, d_=dst, h=half: v.tensor_copy(
                            d_[:, h * 512:(h + 1) * 512], ps8[:, 3584:4096]),
                            waits=[(sT, mtb)], incs=[(sV, 1)])
                        m_mbc_ev = sV.n
            m_diag = [0, 0]
            for e in range(NE):
                for tl in range(4):
                    gt_ = b * NT + tb * 4 + tl
                    dp = tl % 2
                    E.emit("vector", lambda v, e_=e, g2=gt_, dp_=dp: v.tensor_scalar(
                        diag_bf[:, dp_, :], eye_bf[:, :],
                        masks_all[:, NE * g2 + e_:NE * g2 + e_ + 1], None, op0=ALU.mult),
                        waits=[(sV, mark["A_v"] if (e == 0 and tl == 0) else 0),
                               (sT, m_diag[dp]),
                               m_fin1.get((tb - 1, 3), (sV, 0))],
                        incs=[(sV, 1)])
                    mdg = sV.n
                    E.emit("tensor", lambda t_, tl_=tl, dp_=dp: t_.matmul(
                        ps8[:, 3584 + tl_ * 128:3584 + (tl_ + 1) * 128],
                        onesq_bf[:, :], diag_bf[:, dp_, :], start=True, stop=True),
                        waits=[(sV, mdg), (sV, m_mbc_ev if tl < 2 else 0)],
                        incs=[(sT, 1)])
                    m_diag[dp] = sT.n
                E.emit("vector", lambda v, e_=e: v.tensor_copy(mbc[:, e_, :], ps8[:, 3584:4096]),
                       waits=[(sT, m_diag[0]), (sT, m_diag[1])], incs=[(sV, 1)])
                m_mbc_ev = sV.n
            for is_ in range(NIS):
                for ic4 in range(4):
                    icg = is_ * 4 + ic4
                    for e in range(NE):
                        grp = icg * NE + e
                        gpar = grp % 2
                        ii = is_ * NE + e
                        slot = ii % NW1SLOT
                        pg = grp - 2
                        m_prev_half = (m_gelu.get((tb, pg // NE, pg % NE), 0)
                                       if pg >= 0 else 0)
                        for cc in range(CC):
                            w = []
                            if cc == 0:
                                w = [m_w1ld[(tb, ii)],
                                     (sSC, m_prev_half),
                                     m_fin1.get((tb - 1, 0), (sV, 0))]
                            E.emit("tensor", lambda t_, sl_=slot, cc_=cc, ic_=ic4, gp_=gpar, tb_=tb:
                                   t_.matmul(ps8[:, gp_ * 512:(gp_ + 1) * 512],
                                             w1r[:, sl_, cc_ * 512 + ic_ * 128:cc_ * 512 + (ic_ + 1) * 128],
                                             xnT[:, cc_, 8 + tb_ * 512:8 + (tb_ + 1) * 512],
                                             start=(cc_ == 0), stop=(cc_ == CC - 1)),
                                   waits=w, incs=[(sT, 1)] if cc == CC - 1 else ())
                        m_hmm = sT.n
                        m_w1slot[slot] = m_hmm
                        if ic4 == 3 and ii + NW1SLOT < NIS * NE:
                            emit_w1_load(tb, ii + NW1SLOT)
                        E.emit("scalar", lambda s, e_=e, icg_=icg, gp_=gpar: s.activation(
                            gbuf[:, icg_ % 2, e_, :], ps8[:, gp_ * 512:(gp_ + 1) * 512], AF.Gelu,
                            bias=b1_sb[:, e_ * 32 + icg_:e_ * 32 + icg_ + 1]),
                            waits=[(sT, m_hmm),
                                   (sV, m_sel.get((tb, icg - 2), m_sel.get((tb - 1, 30 + (icg % 2)), 0)))],
                            incs=[(sSC, 1)])
                        m_gelu[(tb, icg, e)] = sSC.n
                    mg = m_gelu[(tb, icg, NE - 1)]
                    gp2 = icg % 2
                    E.emit("vector", lambda v, g2=gp2: v.tensor_tensor(
                        selb[:, 0, :], gbuf[:, g2, 0, :], mbc[:, 0, :], op=ALU.mult),
                        waits=[(sSC, mg), (sV, m_mbc_ev),
                               (sV, m_sel.get((tb, icg - 1), m_sel.get((tb - 1, 31), 0)))])
                    E.emit("vector", lambda v, g2=gp2: v.tensor_tensor(
                        selb[:, 1, :], gbuf[:, g2, 1, :], mbc[:, 1, :], op=ALU.mult),
                        incs=[(sV, 1)])
                    ms1_ = sV.n
                    E.emit("vector", lambda v: v.tensor_tensor(
                        selb[:, 0, :], selb[:, 0, :], selb[:, 1, :], op=ALU.add),
                        waits=[(sV, ms1_)], incs=[(sV, 1)])
                    ms2_ = sV.n
                    E.emit("vector", lambda v, g2=gp2: v.tensor_tensor(
                        selb[:, 1, :], gbuf[:, g2, 2, :], mbc[:, 2, :], op=ALU.mult),
                        waits=[(sV, ms2_)], incs=[(sV, 1)])
                    ms3_ = sV.n
                    E.emit("vector", lambda v: v.tensor_tensor(
                        selb[:, 0, :], selb[:, 0, :], selb[:, 1, :], op=ALU.add),
                        waits=[(sV, ms3_)], incs=[(sV, 1)])
                    ms35_ = sV.n
                    E.emit("vector", lambda v, g2=gp2: v.tensor_tensor(
                        selb[:, 1, :], gbuf[:, g2, 3, :], mbc[:, 3, :], op=ALU.mult),
                        waits=[(sV, ms35_)], incs=[(sV, 1)])
                    ms4_ = sV.n
                    E.emit("vector", lambda v, icg_=icg: v.tensor_tensor(
                        interT[:, icg_, :], selb[:, 0, :], selb[:, 1, :], op=ALU.add),
                        waits=[(sV, ms4_), (sT, m_y_done_prev)],
                        incs=[(sV, 1)])
                    m_sel[(tb, icg)] = sV.n
                    drain_fin(2)
            drain_fin(10000)
            m_gelu_all = m_gelu[(tb, 31, NE - 1)]
            for is_ in range(NIS):
                for tt in range(4):
                    for ic4 in range(4):
                        for oh in range(2):
                            w = []
                            if ic4 == 0 and oh == 0:
                                w = [(sV, m_sel[(tb, is_ * 4 + 3)]),
                                     m_wold[(tb, is_)]]
                                if is_ == 0:
                                    w.append(m_fin1.get((tb - 1, tt), (sV, 0)))
                                    w.append((sSC, m_gelu_all))
                                    if tt == 3:
                                        w.append((sV, m_mbc_ev))
                            E.emit("tensor", lambda t_, is2=is_, tt_=tt, ic_=ic4, oh_=oh:
                                   t_.matmul(
                                       ps8[:, tt_ * 1024 + oh_ * 512:tt_ * 1024 + (oh_ + 1) * 512],
                                       interT[:, is2 * 4 + ic_, tt_ * 128:(tt_ + 1) * 128],
                                       wor[:, is2 % 2, ic_ * D + oh_ * 512:ic_ * D + (oh_ + 1) * 512],
                                       start=(is2 == 0 and ic_ == 0),
                                       stop=(is2 == NIS - 1 and ic_ == 3)),
                                   waits=w,
                                   incs=[(sT, 1)] if (tt == 3 and ic4 == 3 and oh == 1) else ())
                m_yis[(tb, is_)] = sT.n
                m_wosl[is_ % 2] = sT.n
                if is_ + 2 < NIS:
                    emit_wo_load(tb, is_ + 2)
            m_y_done_prev = sT.n
            m_y_done = sT.n
            E.emit("scalar", lambda s: s.copy(ar_out[0], ps8[:, 0:1024]),
                   waits=[(sT, m_y_done), m_yeread[0], (sOU[0], sOU[0].n)],
                   incs=[(sSC, 1)])
            m_fin1[(tb, 0)] = (sSC, sSC.n)
            E.emit("scalar", lambda s: s.copy(ar_out[1], ps8[:, 3072:4096]),
                   waits=[m_yeread[1], (sOU[1], sOU[1].n)], incs=[(sSC, 1)])
            m_fin1[(tb, 3)] = (sSC, sSC.n)
            fin_pend[0] = fin_emit(tb, m_y_done)
            if tb == NTB - 1:
                drain_fin(10000)
        mark["xnT_free"] = sT.n
        mark["masksT_free"] = sT.n
        mark["attn_free"] = sV.n
        mark["arena_fin"] = sV.n
        ctx.close()


    for b in range(nb):
        phaseA(b)
        phaseC(b, n_scr_total)
        if b == 0:
            cast_ctx.close()
        phaseB(b, n_scr_total)

    E.emit("sync", None, waits=[(sOU[0], sOU[0].n), (sOU[1], sOU[1].n)])

    with nc.Block() as blk:
        blk.sync(lambda e: E.run("sync", e))
        blk.scalar(lambda e: E.run("scalar", e))
        blk.vector(lambda e: E.run("vector", e))
        blk.tensor(lambda e: E.run("tensor", e))
        blk.gpsimd(lambda e: E.run("gpsimd", e))

    st.close()
    nc.finalize()
    return nc


def kernel(**inputs):
    x = np.asarray(inputs["L0_attn_x"], dtype=np.float32)
    seq = int(x.shape[1])
    nb = x.shape[0] // N_CORES
    packed = pack_inputs(inputs)
    nc = build_nc(seq=seq, nb=nb)
    in_maps = []
    for c in range(N_CORES):
        m = dict(packed)
        m["x"] = np.ascontiguousarray(x[c * nb:(c + 1) * nb].reshape(nb * seq, D))
        in_maps.append(m)
    res = run_bass_kernel_spmd(nc, in_maps, core_ids=list(range(N_CORES)))
    outs = [np.asarray(res.results[c]["out"]).reshape(nb, seq, D) for c in range(N_CORES)]
    return np.concatenate(outs, axis=0)


if __name__ == "__main__":
    build_nc()
    print("build ok")


# version 21
# speedup vs baseline: 1.0645x; 1.0645x over previous
import numpy as np
import contextlib

import concourse.bass as bass
import concourse.mybir as mybir
from concourse.bass_utils import run_bass_kernel_spmd

F32 = mybir.dt.float32
BF16 = mybir.dt.bfloat16
AF = mybir.ActivationFunctionType
ALU = mybir.AluOpType
AX = mybir.AxisListType

N_CORES = 8
D = 1024
CC = 8
NE = 4
NIS = 8
NH = 16
HP = NH // 2
LN1_EPS = 1e-5
LN2_EPS = 1e-12
ENGS = ("sync", "scalar", "vector", "tensor", "gpsimd")


class Sem:
    def __init__(self, handle, name):
        self.h = handle
        self.name = name
        self.n = 0


class Emitter:

    def __init__(self):
        self.ops = {e: [] for e in ENGS}

    def emit(self, eng, fn, waits=(), incs=()):
        ws = tuple((s, v) for (s, v) in waits if v and v > 0)
        for s, k in incs:
            s.n += k
        self.ops[eng].append((fn, ws, tuple(incs)))

    def run(self, eng_name, eng):
        for fn, waits, incs in self.ops[eng_name]:
            for s, v in waits:
                eng.wait_ge(s.h, v)
            instr = fn(eng) if fn is not None else None
            for s, k in incs:
                instr.then_inc(s.h, k)


def pack_inputs(inputs):
    f = np.ascontiguousarray
    d = {}

    def packT(w):
        return f(w.T.reshape(CC, 128, D).transpose(1, 0, 2).reshape(128, CC * D))

    d["qwT"] = packT(np.asarray(inputs["L0_attn_q_w"], np.float32))
    d["kwT"] = packT(np.asarray(inputs["L0_attn_k_w"], np.float32))
    d["vwT"] = packT(np.asarray(inputs["L0_attn_v_w"], np.float32))
    d["owT"] = packT(np.asarray(inputs["L0_attn_out_w"], np.float32))
    d["qb"] = f(np.asarray(inputs["L0_attn_q_b"], np.float32).reshape(CC, 128).T)
    d["kb"] = f(np.asarray(inputs["L0_attn_k_b"], np.float32).reshape(CC, 128).T)
    cw = np.asarray(inputs["L0_attn_conv_w"], np.float32).reshape(D, 3)
    d["convw"] = f(cw.reshape(CC, 128, 3).transpose(1, 0, 2).reshape(128, CC * 3))
    d["convb"] = f(np.asarray(inputs["L0_attn_conv_b"], np.float32).reshape(CC, 128).T)
    wr = np.asarray(inputs["L0_ffn_W_router"], np.float32)
    d["wr"] = f(wr.reshape(CC, 128, NE).transpose(1, 0, 2).reshape(128, CC * NE))
    w1 = np.asarray(inputs["L0_ffn_W1"], np.float32)
    d["w1"] = f(w1.reshape(NE, CC, 128, NIS, 512).transpose(0, 3, 2, 1, 4)
                .reshape(NE * NIS, 128, CC * 512))
    b1 = np.asarray(inputs["L0_ffn_b1"], np.float32)
    d["b1"] = f(b1.reshape(NE, 32, 128).transpose(2, 0, 1).reshape(128, NE * 32))
    wo = np.asarray(inputs["L0_ffn_Wo"], np.float32)
    d["wo"] = f(wo.reshape(NIS, 4, 128, D).transpose(0, 2, 1, 3).reshape(NIS, 128, 4 * D))
    d["rows"] = f(np.stack([
        np.asarray(inputs["L0_ffn_ln_gamma"], np.float32),
        np.asarray(inputs["L0_ffn_ln_beta"], np.float32),
        np.asarray(inputs["L0_attn_out_b"], np.float32),
        np.asarray(inputs["L0_ffn_bo"], np.float32),
        np.asarray(inputs["L0_attn_v_b"], np.float32),
    ], axis=0))
    d["eye"] = np.eye(128, dtype=np.float32)
    return d


def build_nc(seq=2048, nb=2, debug=False):
    NT = seq // 128
    KEYS = seq // 2
    NKC = KEYS // 128
    NKB = max(1, KEYS // 512)
    NQB = seq // 512
    NTB = NQB
    XCOL = seq + 8

    nc = bass.Bass("TRN2", target_bir_lowering=False, debug=debug)

    x_d = nc.dram_tensor("x", [nb * seq, D], F32, kind="ExternalInput").ap()
    out_d = nc.dram_tensor("out", [nb * seq, D], F32, kind="ExternalOutput").ap()
    qwT_d = nc.dram_tensor("qwT", [128, CC * D], F32, kind="ExternalInput").ap()
    kwT_d = nc.dram_tensor("kwT", [128, CC * D], F32, kind="ExternalInput").ap()
    vwT_d = nc.dram_tensor("vwT", [128, CC * D], F32, kind="ExternalInput").ap()
    owT_d = nc.dram_tensor("owT", [128, CC * D], F32, kind="ExternalInput").ap()
    qb_d = nc.dram_tensor("qb", [128, CC], F32, kind="ExternalInput").ap()
    kb_d = nc.dram_tensor("kb", [128, CC], F32, kind="ExternalInput").ap()
    convw_d = nc.dram_tensor("convw", [128, CC * 3], F32, kind="ExternalInput").ap()
    convb_d = nc.dram_tensor("convb", [128, CC], F32, kind="ExternalInput").ap()
    wr_d = nc.dram_tensor("wr", [128, CC * NE], F32, kind="ExternalInput").ap()
    w1_d = nc.dram_tensor("w1", [NE * NIS, 128, CC * 512], F32, kind="ExternalInput").ap()
    b1_d = nc.dram_tensor("b1", [128, NE * 32], F32, kind="ExternalInput").ap()
    wo_d = nc.dram_tensor("wo", [NIS, 128, 4 * D], F32, kind="ExternalInput").ap()
    rows_d = nc.dram_tensor("rows", [5, D], F32, kind="ExternalInput").ap()
    scrW1 = nc.dram_tensor("scrW1", [NE * NIS, 128, CC * 512], BF16, kind="Internal").ap()
    scrWo = nc.dram_tensor("scrWo", [NIS, 128, 4 * D], BF16, kind="Internal").ap()
    eye_d = nc.dram_tensor("eye", [128, 128], F32, kind="ExternalInput").ap()

    st = contextlib.ExitStack()

    def sb(name, shape, dt):
        return st.enter_context(nc.sbuf_tensor(name, shape, dt))

    def mksem(name):
        return Sem(st.enter_context(nc.semaphore(name)), name)

    sCONST = mksem("sCONST")
    sCAST = [mksem(f"sCAST{i}") for i in range(2)]
    sCSTO = [mksem(f"sCSTO{i}") for i in range(2)]
    sSCR = [mksem(f"sSCR{i}") for i in range(2)]
    sQW = mksem("sQW")
    sX = [mksem(f"sX{i}") for i in range(2)]
    sROW = mksem("sROW")
    sOU = [mksem(f"sOU{i}") for i in range(2)]
    sW1 = [mksem(f"sW1_{i}") for i in range(4)]
    sWo = [mksem(f"sWo{i}") for i in range(2)]
    sW1g = [mksem(f"sW1g{i}") for i in range(4)]
    sWog = [mksem(f"sWog{i}") for i in range(2)]
    sT = mksem("sT")
    sSC = mksem("sSC")
    sV = mksem("sV")

    E = Emitter()

    eye_f = sb("eye_f", [128, 128], F32)
    ones_f = sb("ones_f", [1, 128], F32)
    ones_bf = sb("ones_bf", [1, 128], BF16)
    qb_sb = sb("qb_sb", [128, CC], F32)
    kb_sb = sb("kb_sb", [128, CC], F32)
    b1_sb = sb("b1_sb", [128, NE * 32], F32)
    convw_sb = sb("convw_sb", [128, CC * 3], F32)
    convb_sb = sb("convb_sb", [128, CC], F32)
    wr_f = sb("wr_f", [128, CC * NE], F32)
    vb_bf = sb("vb_bf", [1, D], BF16)
    eye_bf = sb("eye_bf", [128, 128], BF16)
    onesq_bf = sb("onesq_bf", [128, 128], BF16)
    murstd = sb("murstd", [128, 2 * nb * NT], F32)
    masks_all = sb("masks_all", [128, nb * NT * NE], F32)
    xnT = sb("xnT", [128, CC, XCOL], BF16)
    attn_sb = sb("attn_sb", [128, NT, D], BF16)
    arena = sb("arena", [128, 6 * D], F32)
    stats = sb("stats", [128, 16], F32)
    cst = sb("cst", [128, 2], F32)
    lgt = sb("lgt", [128, 8], F32)

    ar_xbuf = [arena[:, 0 * D:1 * D], arena[:, 1 * D:2 * D]]
    ar_xn = [arena[:, 2 * D:3 * D], arena[:, 3 * D:4 * D]]
    ar_fa = arena[:, 2 * D:3 * D]
    ar_fb = arena[:, 3 * D:4 * D]
    ar_out = [arena[:, 4 * D:5 * D], arena[:, 5 * D:6 * D]]

    mark = {
        "xnT_free": 0,
        "masksT_free": 0,
        "attn_free": 0,
        "arena_fin": 0,
        "ones_v": 0,
        "A_sc": 0,
        "A_v": 0,
        "wslot": 0,
    }

    def barrier(extra=()):
        t, sc, v = sT.n, sSC.n, sV.n
        for eng in ENGS:
            w = []
            if eng != "tensor":
                w.append((sT, t))
            if eng != "scalar":
                w.append((sSC, sc))
            if eng != "vector":
                w.append((sV, v))
            w += list(extra)
            E.emit(eng, None, waits=w)

    def startup():
        sctx = contextlib.ExitStack()
        for dst, src in [(eye_f, eye_d), (qb_sb, qb_d), (kb_sb, kb_d), (b1_sb, b1_d),
                         (convw_sb, convw_d), (convb_sb, convb_d), (wr_f, wr_d)]:
            E.emit("gpsimd", lambda g, d_=dst, s_=src: g.dma_start(d_[:, :], s_[:, :]),
                   incs=[(sCONST, 16)])
        E.emit("gpsimd", lambda g: g.dma_start(vb_bf[:, :], rows_d[4:5, :]), incs=[(sCONST, 16)])
        E.emit("gpsimd", lambda g: g.dma_start(eye_bf[:, :], eye_d[:, :]), incs=[(sCONST, 16)])
        ld_consts = sCONST.n
        mark["consts"] = ld_consts
        E.emit("vector", lambda v: v.memset(onesq_bf[:, :], 1.0))
        E.emit("vector", lambda v: v.memset(ones_f[:, :], 1.0))
        E.emit("vector", lambda v: v.memset(ones_bf[:, :], 1.0))
        E.emit("vector", lambda v: v.memset(xnT[:, :, 0:8], 0.0))
        E.emit("vector", lambda v: v.memset(cst[:, 0:1], LN1_EPS))
        E.emit("vector", lambda v: v.memset(cst[:, 1:2], LN2_EPS), incs=[(sV, 1)])
        mark["ones_v"] = sV.n
        sctx.close()
        return (0, 0)

    n_scr_total = startup()

    cast_ctx = contextlib.ExitStack()
    cslot = cast_ctx.enter_context(nc.sbuf_tensor("castslot", [128, 2, 1024], BF16))
    mark["scr_pieces"] = {}
    cast_state = {"chain": [(sCSTO[0], 0), (sCSTO[1], 0)], "idx": 0, "ops": []}
    for piece in range(NE * NIS):
        for half in range(4):
            sl = half % 2
            srcap = w1_d[piece, :, half * 1024:(half + 1) * 1024]
            dstap = scrW1[piece, :, half * 1024:(half + 1) * 1024]
            cast_state["ops"].append((piece, sl, srcap, dstap))

    def drain_cast(k):
        ops, chain = cast_state["ops"], cast_state["chain"]
        for _ in range(k):
            if cast_state["idx"] >= len(ops):
                return
            piece, sl, srcap, dstap = ops[cast_state["idx"]]
            cast_state["idx"] += 1
            E.emit("gpsimd", lambda g, sl_=sl, s_=srcap: g.dma_start(
                cslot[:, sl_, :], s_),
                waits=[chain[sl], (sCONST, mark["consts"])], incs=[(sCAST[sl], 16)])
            mld = sCAST[sl].n
            E.emit("sync", lambda s, sl_=sl, d_=dstap: s.dma_start(
                d_, cslot[:, sl_, :]),
                waits=[(sCAST[sl], mld)], incs=[(sCSTO[sl], 16)])
            chain[sl] = (sCSTO[sl], sCSTO[sl].n)
            if cast_state["idx"] % 4 == 0:
                mark["scr_pieces"][piece] = (chain[0], chain[1])


    def phaseA(b):
        barrier(extra=[(sCONST, mark["consts"])])
        ctxA = contextlib.ExitStack()
        ptr0 = ctxA.enter_context(nc.psum_tensor(f"psA_tr0_{b}", [128, 128], F32))
        ptr1 = ctxA.enter_context(nc.psum_tensor(f"psA_tr1_{b}", [128, 128], F32))
        ps_r = ctxA.enter_context(nc.psum_tensor(f"psA_r_{b}", [128, 2 * NE], F32))
        xnT32 = ctxA.enter_context(nc.sbuf_tensor(f"xnT32_{b}", [128, CC, 128], F32))
        dump_t = ctxA.enter_context(nc.sbuf_tensor(f"dumpA_{b}", [128, D], F32))
        ptr = [ptr0, ptr1]
        m_masks_prev = 0
        m_masks_par = [0, 0]
        m_evac = [0] * (2 * CC)
        NTOT = nb * NT
        xsl = [arena[:, 0:2 * D], arena[:, 4 * D:6 * D]]
        xn = arena[:, 2 * D:4 * D]
        m_xslot_free = [0, 0]
        m_stats_free = [0, 0]
        m_tr_done = 0
        NP = NT // 2
        m_lgt_prev = [0]
        m_router_prev = [0]
        m_evacS = [0] * (2 * CC)
        m_evacV = [0] * (2 * CC)
        loads = {}
        ms0s = {}
        ms2s = {}
        mxns = {}
        mlgts = {}
        trd = {}

        def emit_load(k):
            par = k % 2
            g0 = b * NT + 2 * k
            E.emit("sync", lambda s, g_=g0, p_=par: s.dma_start(
                xsl[p_].rearrange("p (a d) -> p a d", a=2),
                x_d[g_ * 128:(g_ + 2) * 128, :].rearrange("(a p) d -> p a d", p=128)),
                waits=[(sV, m_xslot_free[par]), (sV, mark["arena_fin"]),
                       (sOU[0], sOU[0].n), (sOU[1], sOU[1].n)],
                incs=[(sX[par], 16)])
            loads[k] = sX[par].n

        def emit_accums(k):
            par = k % 2
            so = 8 * par
            xb = xsl[par]
            for a in range(2):
                w0 = ([(sX[par], loads[k]), (sV, m_stats_free[par]),
                       (sV, mark["arena_fin"] if k < 2 else 0),
                       (sT, mark["xnT_free"] if k < 2 else 0)]
                      if a == 0 else [(sSC, sSC.n)])
                E.emit("scalar", lambda s, xb_=xb, a_=a, so_=so: s.activation(
                    dump_t[:, :], xb_[:, a_ * D:(a_ + 1) * D], AF.Identity,
                    accum_out=stats[:, so_ + a_:so_ + a_ + 1]),
                    waits=w0, incs=[(sSC, 1)])
                E.emit("scalar", lambda s, xb_=xb, a_=a, so_=so: s.activation(
                    dump_t[:, :], xb_[:, a_ * D:(a_ + 1) * D], AF.Square,
                    accum_out=stats[:, so_ + 2 + a_:so_ + 3 + a_]),
                    waits=[(sSC, sSC.n)], incs=[(sSC, 1)])
            ms0s[k] = sSC.n

        def emit_smalls_lnexp(k):
            par = k % 2
            so = 8 * par
            g0 = b * NT + 2 * k
            E.emit("vector", lambda v, so_=so, g_=g0: v.tensor_scalar(
                murstd[:, g_:g_ + 2], stats[:, so_:so_ + 2], 1.0 / D, None, op0=ALU.mult),
                waits=[(sSC, ms0s[k])], incs=[(sV, 1)])
            mv4 = sV.n
            E.emit("vector", lambda v, so_=so: v.tensor_scalar(
                stats[:, so_ + 4:so_ + 6], stats[:, so_ + 2:so_ + 4], 1.0 / D, None,
                op0=ALU.mult),
                waits=[(sSC, ms0s[k])], incs=[(sV, 1)])
            m_stats_free[par] = sV.n
            E.emit("vector", lambda v, so_=so, g_=g0: v.tensor_tensor(
                stats[:, so_ + 6:so_ + 8], murstd[:, g_:g_ + 2], murstd[:, g_:g_ + 2],
                op=ALU.mult),
                waits=[(sV, mv4)], incs=[(sV, 1)])
            mv6 = sV.n
            E.emit("vector", lambda v, so_=so: v.tensor_tensor(
                stats[:, so_ + 4:so_ + 6], stats[:, so_ + 4:so_ + 6],
                stats[:, so_ + 6:so_ + 8], op=ALU.subtract),
                waits=[(sV, mv6)], incs=[(sV, 1)])
            mv7 = sV.n
            E.emit("scalar", lambda s, so_=so: s.activation(
                stats[:, so_ + 6:so_ + 8], stats[:, so_ + 4:so_ + 6], AF.Ln, bias=cst[:, 0:1]),
                waits=[(sV, mv7), (sV, m_masks_par[par])], incs=[(sSC, 1)])
            ms1 = sSC.n
            E.emit("scalar", lambda s, so_=so, g_=g0: s.activation(
                murstd[:, NTOT + g_:NTOT + g_ + 2], stats[:, so_ + 6:so_ + 8],
                AF.Exp, scale=-0.5),
                waits=[(sSC, ms1)], incs=[(sSC, 1)])
            ms2s[k] = sSC.n

        def emit_xn(k):
            par = k % 2
            g0 = b * NT + 2 * k
            xb = xsl[par]
            for a in range(2):
                E.emit("vector", lambda v, xb_=xb, a_=a, g_=g0 + a: v.tensor_scalar(
                    xn[:, a_ * D:(a_ + 1) * D], xb_[:, a_ * D:(a_ + 1) * D],
                    murstd[:, g_:g_ + 1], murstd[:, NTOT + g_:NTOT + g_ + 1],
                    op0=ALU.subtract, op1=ALU.mult),
                    waits=[(sSC, ms2s[k]) if a == 0 else (sV, 0),
                           (sT, trd.get(k - 1, 0)) if a == 0 else (sV, 0),
                           (sV, mark["arena_fin"] if k < 2 else 0)],
                    incs=[(sV, 1)])
            mxns[k] = sV.n
            m_xslot_free[par] = mxns[k]

        def emit_tr(k):
            t0 = 2 * k
            m_router_prev2 = 0
            last_tr = 0
            for a in range(2):
                t = t0 + a
                for cc in range(CC):
                    ci = a * CC + cc
                    tp = ci % 2
                    E.emit("tensor", lambda t_, cc_=cc, tp_=tp, a_=a: t_.transpose(
                        ptr[tp_][:, :], xn[:, a_ * D + cc_ * 128:a_ * D + (cc_ + 1) * 128],
                        eye_f[:, :]),
                        waits=[(sV, mxns[k]), (sV, m_evacV[ci - 2]),
                               (sT, mark["xnT_free"] if (k == 0 and ci == 0) else 0)],
                        incs=[(sT, 1)])
                    mtr = sT.n
                    last_tr = mtr
                    E.emit("vector", lambda v, cc_=cc, tp_=tp: v.tensor_copy(
                        xnT32[:, cc_, :], ptr[tp_][:, :]),
                        waits=[(sT, mtr), (sT, m_router_prev2 or m_router_prev[0]),
                               (sSC, m_evacS[(ci + 8) % 16])],
                        incs=[(sV, 1)])
                    m_evacV[ci] = sV.n
                    E.emit("scalar", lambda s, cc_=cc, t_=t: s.copy(
                        xnT[:, cc_, 8 + t_ * 128:8 + (t_ + 1) * 128], xnT32[:, cc_, :]),
                        waits=[(sV, m_evacV[ci]),
                               (sT, mark["xnT_free"] if (k == 0 and ci == 0) else 0)],
                        incs=[(sSC, 1)])
                    m_evacS[ci] = sSC.n
                for cc in range(CC):
                    ci = a * CC + cc
                    w = [(sV, m_evacV[ci])]
                    if cc == 0:
                        w.append((sV, m_lgt_prev[0]))
                    E.emit("tensor", lambda t_, cc_=cc, a_=a, f_=(cc == 0), l_=(cc == CC - 1):
                           t_.matmul(ps_r[:, 4 * a_:4 * a_ + 4], xnT32[:, cc_, :],
                                     wr_f[:, cc_ * NE:(cc_ + 1) * NE], start=f_, stop=l_),
                           waits=w, incs=[(sT, 1)] if cc == CC - 1 else ())
                if a == 0:
                    m_router_prev2 = sT.n
            m_router_prev[0] = sT.n
            trd[k] = last_tr
            E.emit("vector", lambda v: v.tensor_copy(lgt[:, 0:8], ps_r[:, :]),
                   waits=[(sT, m_router_prev[0])], incs=[(sV, 1)])
            mlgts[k] = sV.n
            m_lgt_prev[0] = sV.n

        def emit_masks(k):
            nonlocal m_masks_prev
            par = k % 2
            so = 8 * par
            g0 = b * NT + 2 * k
            mo = NE * g0
            mprev = 0
            for e in range(NE):
                js = [j for j in range(NE) if j != e]
                ops = [ALU.is_gt if j < e else ALU.is_ge for j in js]
                w0 = [(sV, mlgts[k]), (sV, mprev), (sSC, ms2s[k])]
                if k == 0 and e == 0:
                    w0.append((sT, mark["masksT_free"]))
                E.emit("vector", lambda v, e_=e, mo_=mo, j_=js[0], o_=ops[0]: v.tensor_tensor(
                    masks_all[:, mo_ + e_:mo_ + e_ + 5:4], lgt[:, e_:e_ + 5:4],
                    lgt[:, j_:j_ + 5:4], op=o_),
                    waits=w0)
                E.emit("vector", lambda v, e_=e, so_=so, j_=js[1], o_=ops[1]: v.tensor_tensor(
                    stats[:, so_ + 4:so_ + 6], lgt[:, e_:e_ + 5:4],
                    lgt[:, j_:j_ + 5:4], op=o_))
                E.emit("vector", lambda v, e_=e, so_=so, j_=js[2], o_=ops[2]: v.tensor_tensor(
                    stats[:, so_ + 6:so_ + 8], lgt[:, e_:e_ + 5:4],
                    lgt[:, j_:j_ + 5:4], op=o_),
                    incs=[(sV, 1)])
                mc = sV.n
                E.emit("vector", lambda v, so_=so: v.tensor_tensor(
                    stats[:, so_ + 4:so_ + 6], stats[:, so_ + 4:so_ + 6],
                    stats[:, so_ + 6:so_ + 8], op=ALU.mult),
                    waits=[(sV, mc)], incs=[(sV, 1)])
                mc2 = sV.n
                E.emit("vector", lambda v, e_=e, mo_=mo, so_=so: v.tensor_tensor(
                    masks_all[:, mo_ + e_:mo_ + e_ + 5:4], masks_all[:, mo_ + e_:mo_ + e_ + 5:4],
                    stats[:, so_ + 4:so_ + 6], op=ALU.mult),
                    waits=[(sV, mc2)], incs=[(sV, 1)])
                mprev = sV.n
            m_masks_prev = sV.n
            m_masks_par[par] = sV.n

        for kpair in range(NP):
            if kpair == 0:
                emit_load(0)
                emit_accums(0)
            emit_smalls_lnexp(kpair)
            if kpair + 1 < NP:
                emit_load(kpair + 1)
                emit_accums(kpair + 1)
            emit_xn(kpair)
            if kpair >= 1:
                emit_masks(kpair - 1)
            emit_tr(kpair)
            if b == 0:
                drain_cast(3)
        emit_masks(NP - 1)
        ctxA.close()
        mark["A_sc"] = sSC.n
        mark["A_v"] = sV.n

    def phaseC(b, n_scr_total):
        barrier()
        ctx = contextlib.ExitStack()
        qT = ctx.enter_context(nc.sbuf_tensor(f"qT{b}", [128, CC, seq], BF16))
        kT = ctx.enter_context(nc.sbuf_tensor(f"kT{b}", [128, CC, KEYS], BF16))
        v_sb = ctx.enter_context(nc.sbuf_tensor(f"v{b}", [128, NKC, NH, 65], BF16))
        wslot = ctx.enter_context(nc.sbuf_tensor(f"wslot{b}", [128, CC * D], BF16))
        ctx2 = contextlib.ExitStack()
        xdT = ctx2.enter_context(nc.sbuf_tensor(f"xdT{b}", [128, CC, KEYS], BF16))
        m_A = mark["A_sc"]
        dc = drain_cast if b == 0 else (lambda k: None)

        def load_w(src):
            E.emit("gpsimd", lambda g_, s_=src: g_.dma_start(wslot[:, :], s_[:, :]),
                   waits=[(sT, mark["wslot"])], incs=[(sQW, 16)])
            return sQW.n

        m_conv = 0
        for cc in range(CC):
            t1 = xnT[:, cc, 7:7 + 2 * (KEYS - 1) + 1:2]
            t2 = xnT[:, cc, 8:8 + 2 * (KEYS - 1) + 1:2]
            t3 = xnT[:, cc, 9:9 + 2 * (KEYS - 1) + 1:2]
            E.emit("vector", lambda v, t_=t1, cc_=cc: v.tensor_scalar(
                ar_fa[:, 0:KEYS], t_, convw_sb[:, 3 * cc_:3 * cc_ + 1], None, op0=ALU.mult),
                waits=[(sSC, m_A), (sV, mark["arena_fin"]), (sV, m_conv)],
                incs=[(sV, 1)])
            mvc = sV.n
            E.emit("vector", lambda v, t_=t2, cc_=cc: v.scalar_tensor_tensor(
                ar_fb[:, 0:KEYS], t_, convw_sb[:, 3 * cc_ + 1:3 * cc_ + 2], ar_fa[:, 0:KEYS],
                op0=ALU.mult, op1=ALU.add),
                waits=[(sV, mvc)], incs=[(sV, 1)])
            mvc = sV.n
            E.emit("vector", lambda v, t_=t3, cc_=cc: v.scalar_tensor_tensor(
                ar_fa[:, 0:KEYS], t_, convw_sb[:, 3 * cc_ + 2:3 * cc_ + 3], ar_fb[:, 0:KEYS],
                op0=ALU.mult, op1=ALU.add),
                waits=[(sV, mvc)], incs=[(sV, 1)])
            mvc = sV.n
            E.emit("vector", lambda v, cc_=cc: v.tensor_scalar(
                xdT[:, cc_, :], ar_fa[:, 0:KEYS], convb_sb[:, cc_:cc_ + 1], None, op0=ALU.add),
                waits=[(sV, mvc)], incs=[(sV, 1)])
            m_conv = sV.n
        m_conv_all = m_conv

        with (nc.psum_tensor(f"psC0{b}", [128, 512], F32) as pc0,
              nc.psum_tensor(f"psC1{b}", [128, 512], F32) as pc1):
            pc = [pc0, pc1]
            m_ev = [0, 0]
            kmm = 0
            mq = load_w(qwT_d)
            for oc in range(CC):
                for tb in range(NQB):
                    pp = kmm % 2
                    kmm += 1
                    for cc in range(CC):
                        w = [(sQW, mq), (sSC, m_A), (sSC, m_ev[pp])] if cc == 0 else []
                        E.emit("tensor", lambda t_, oc_=oc, tb_=tb, cc_=cc, pp_=pp: t_.matmul(
                            pc[pp_][:, :],
                            wslot[:, cc_ * D + oc_ * 128:cc_ * D + (oc_ + 1) * 128],
                            xnT[:, cc_, 8 + tb_ * 512:8 + (tb_ + 1) * 512],
                            start=(cc_ == 0), stop=(cc_ == CC - 1)),
                            waits=w, incs=[(sT, 1)] if cc == CC - 1 else ())
                    mmm = sT.n
                    E.emit("scalar", lambda s, oc_=oc, tb_=tb, pp_=pp: s.activation(
                        qT[:, oc_, tb_ * 512:(tb_ + 1) * 512], pc[pp_][:, :], AF.Identity,
                        bias=qb_sb[:, oc_:oc_ + 1]),
                        waits=[(sT, mmm)], incs=[(sSC, 1)])
                    m_ev[pp] = sSC.n
            mark["wslot"] = sT.n
            mk_ = load_w(kwT_d)
            for oc in range(CC):
                for kb in range(NKB):
                    klen = min(512, KEYS - kb * 512)
                    pp = kmm % 2
                    kmm += 1
                    for cc in range(CC):
                        w = [(sQW, mk_), (sV, m_conv_all), (sSC, m_ev[pp])] if cc == 0 else []
                        E.emit("tensor", lambda t_, oc_=oc, kb_=kb, cc_=cc, pp_=pp, kl=klen:
                               t_.matmul(pc[pp_][:, 0:kl],
                                         wslot[:, cc_ * D + oc_ * 128:cc_ * D + (oc_ + 1) * 128],
                                         xdT[:, cc_, kb_ * 512:kb_ * 512 + kl],
                                         start=(cc_ == 0), stop=(cc_ == CC - 1)),
                               waits=w, incs=[(sT, 1)] if cc == CC - 1 else ())
                    mmm = sT.n
                    E.emit("scalar", lambda s, oc_=oc, kb_=kb, pp_=pp, kl=klen: s.activation(
                        kT[:, oc_, kb_ * 512:kb_ * 512 + kl], pc[pp_][:, 0:kl], AF.Identity,
                        bias=kb_sb[:, oc_:oc_ + 1]),
                        waits=[(sT, mmm)], incs=[(sSC, 1)])
                    m_ev[pp] = sSC.n
            mark["wslot"] = sT.n
            mv_ = load_w(vwT_d)
            E.emit("vector", lambda v: v.memset(v_sb[:, :, :, 64:65], 1.0),
                   waits=[(sV, mark["attn_free"])], incs=[(sV, 1)])
            m_vones = sV.n
            m_vev = [0, 0]
            for kt in range(NKC):
                for oh in range(2):
                    pp = kmm % 2
                    kmm += 1
                    for cc in range(CC):
                        w = ([(sQW, mv_), (sV, m_conv_all), (sSC, m_ev[pp]),
                              (sV, m_vev[pp])] if cc == 0 else [])
                        E.emit("tensor", lambda t_, kt_=kt, oh_=oh, cc_=cc, pp_=pp: t_.matmul(
                            pc[pp_][:, :], xdT[:, cc_, kt_ * 128:(kt_ + 1) * 128],
                            wslot[:, cc_ * D + oh_ * 512:cc_ * D + (oh_ + 1) * 512],
                            start=(cc_ == 0), stop=False),
                            waits=w)
                    E.emit("tensor", lambda t_, oh_=oh, pp_=pp: t_.matmul(
                        pc[pp_][:, :], ones_bf[0:1, :], vb_bf[0:1, oh_ * 512:(oh_ + 1) * 512],
                        start=False, stop=True),
                        incs=[(sT, 1)])
                    mmm = sT.n
                    E.emit("vector", lambda v, kt_=kt, oh_=oh, pp_=pp: v.tensor_copy(
                        v_sb[:, kt_, 8 * oh_:8 * (oh_ + 1), 0:64],
                        pc[pp_][:, :].rearrange("p (h c) -> p h c", c=64)),
                        waits=[(sT, mmm), (sV, m_vones)], incs=[(sV, 1)])
                    m_vev[pp] = sV.n
            mark["wslot"] = sT.n
            m_v_done = sV.n
            m_kqv_sc = sSC.n
        ctx2.close()
        otp = ctx.enter_context(nc.sbuf_tensor(f"otp{b}", [128, HP, 512], BF16))
        rrow_t = ctx.enter_context(nc.sbuf_tensor(f"rrow{b}", [1, 512], F32))
        rcbc_t = ctx.enter_context(nc.sbuf_tensor(f"rcbc{b}", [64, 512], F32))
        rrows = [rrow_t[0:1, :], ar_fa[0:1, 0:512]]
        rcbc = rcbc_t[:, :]
        etb_t = ctx.enter_context(nc.sbuf_tensor(f"etb{b}", [128, 2 * NKC * 512], BF16))
        mo_ = load_w(owT_d)
        NOT = 3
        with (nc.psum_tensor(f"psS0{b}", [128, 1024], F32) as ps_s0,
              nc.psum_tensor(f"psS1{b}", [128, 1024], F32) as ps_s1,
              nc.psum_tensor(f"psOT0{b}", [128, 512], F32) as ps_ot0,
              nc.psum_tensor(f"psOT1{b}", [128, 512], F32) as ps_ot1,
              nc.psum_tensor(f"psOT2{b}", [128, 512], F32) as ps_ot2,
              nc.psum_tensor(f"psBC{b}", [128, 512], F32) as ps_bc):
            ps_s = [ps_s0, ps_s1]
            ring = [ps_ot0, ps_ot1, ps_ot2]
            m_ring = [None, None, None]
            rslot = [0]
            m_exp = {}
            m_ot = {}
            m_otevac = {}
            m_rc = 0
            m_rcbc = 0
            m_atp_ev = 0
            m_otp_free = 0

            def take_slot():
                r = rslot[0] % NOT
                rslot[0] += 1
                return r

            def prev_h(qb, h, k):
                hh = qb * NH + h - k
                return (hh // NH, hh % NH) if hh >= 0 else None

            def emit_ot_pair(pend, pair):
                (pqb, ph_), pr = pend
                for kc in (2 * pair, 2 * pair + 1):
                    w = [(sSC, m_exp[((pqb, ph_), pair)])]
                    if kc == 0 and m_ring[pr] is not None:
                        w.append(m_ring[pr])
                    if kc == 0 and (pqb, ph_) == (0, 0):
                        w.append((sV, m_v_done))
                    eo = (ph_ % 2) * NKC * 512
                    E.emit("tensor", lambda t_, h_=ph_, kc_=kc, r_=pr, eo_=eo, q_=pqb:
                           t_.matmul(ring[r_][0:65, :], v_sb[:, kc_, h_, :],
                                     etb_t[:, eo_ + kc_ * 512:eo_ + (kc_ + 1) * 512],
                                     start=(kc_ == 0), stop=(kc_ == NKC - 1)),
                           waits=w, incs=[(sT, 1)] if kc == NKC - 1 else ())
                if pair == NKC // 2 - 1:
                    m_ot[(pqb, ph_)] = sT.n

            m_recip = {}
            m_bcpar = [0, 0]

            def sigma_recip(pend):
                (pqb, ph_), pr = pend
                sl = (pqb * NH + ph_) % 2
                E.emit("vector", lambda v, r_=pr, sl_=sl: v.reciprocal(
                    rrows[sl_], ring[r_][64:65, :]),
                    waits=[(sT, m_ot[(pqb, ph_)]), (sT, m_bcpar[sl])], incs=[(sV, 1)])
                m_recip[(pqb, ph_)] = sV.n

            def sigma_rest(pend):
                nonlocal m_rcbc, m_otp_free
                (pqb, ph_), pr = pend
                sl = (pqb * NH + ph_) % 2
                hp2, hh2 = ph_ // 2, ph_ % 2
                E.emit("tensor", lambda t_, sl_=sl: t_.matmul(
                    ps_bc[0:64, :], ones_f[0:1, 0:64], rrows[sl_],
                    start=True, stop=True),
                    waits=[(sV, m_recip[(pqb, ph_)]), (sV, m_rcbc)], incs=[(sT, 1)])
                mbc = sT.n
                m_bcpar[sl] = mbc
                E.emit("vector", lambda v: v.tensor_copy(rcbc, ps_bc[0:64, :]),
                       waits=[(sT, mbc), (sV, m_rcbc)], incs=[(sV, 1)])
                mrcp = sV.n
                E.emit("vector", lambda v, a=hp2, b2=hh2, r_=pr: v.tensor_tensor(
                    otp[64 * b2:64 * (b2 + 1), a, :], ring[r_][0:64, :], rcbc,
                    op=ALU.mult),
                    waits=[(sV, mrcp), (sT, m_otp_free)], incs=[(sV, 1)])
                m_otevac[(pqb, ph_)] = sV.n
                m_ring[pr] = (sV, sV.n)
                m_rcbc = sV.n

            def emit_outproj(pqb):
                nonlocal m_atp_ev, m_otp_free
                m_all_otp = m_otevac[(pqb, NH - 1)]
                for tt in range(4):
                    gt = pqb * 4 + tt
                    r1, r2 = take_slot(), take_slot()
                    for oh, rr in ((0, r1), (1, r2)):
                        for oc in range(CC):
                            w = []
                            if oc == 0:
                                w = [(sV, m_all_otp), (sQW, mo_)]
                                if m_ring[rr] is not None:
                                    w.append(m_ring[rr])
                            E.emit("tensor", lambda t_, oc_=oc, oh_=oh, tt_=tt, r_=rr: t_.matmul(
                                ring[r_][:, :],
                                otp[:, oc_, tt_ * 128:(tt_ + 1) * 128],
                                wslot[:, oc_ * D + oh_ * 512:oc_ * D + (oh_ + 1) * 512],
                                start=(oc_ == 0), stop=(oc_ == CC - 1)),
                                waits=w,
                                incs=[(sT, 1)] if oc == CC - 1 else ())
                    mat = sT.n
                    for oh, rr in ((0, r1), (1, r2)):
                        E.emit("scalar", lambda s, gt_=gt, oh_=oh, r_=rr: s.copy(
                            attn_sb[:, gt_, oh_ * 512:(oh_ + 1) * 512], ring[r_][:, :]),
                            waits=[(sT, mat), (sV, mark["attn_free"])], incs=[(sSC, 1)])
                        m_ring[rr] = (sSC, sSC.n)
                    m_atp_ev = sSC.n
                m_otp_free = sT.n

            pend = None
            pend2 = None
            npair = [0]
            pair_exp = {}
            for qb in range(NQB):
                for h in range(NH):
                    hp_, hh = h // 2, h % 2
                    eoff = (h % 2) * NKC * 512
                    for pair in range(NKC // 2):
                        if pend is not None:
                            emit_ot_pair(pend, pair)
                        gp = npair[0]
                        npair[0] += 1
                        spar = gp % 2
                        for j in range(2):
                            w = []
                            if j == 0:
                                if gp >= 2:
                                    w.append((sSC, pair_exp[gp - 2]))
                                else:
                                    w.append((sSC, m_kqv_sc))
                                    w.append((sV, m_v_done))
                            E.emit("tensor", lambda t_, hp2=hp_, hh2=hh, p_=pair, qb_=qb, par_=spar, j_=j:
                                   t_.matmul(ps_s[par_][:, j_ * 512:(j_ + 1) * 512],
                                             kT[64 * hh2:64 * (hh2 + 1), hp2,
                                                (2 * p_ + j_) * 128:(2 * p_ + j_ + 1) * 128],
                                             qT[64 * hh2:64 * (hh2 + 1), hp2, qb_ * 512:(qb_ + 1) * 512],
                                             start=True, stop=True),
                                   waits=w, incs=[(sT, 1)])
                        msp = sT.n
                        ph = prev_h(qb, h, 2)
                        we = [(sT, msp), (sT, m_ot[ph] if ph else 0)]
                        E.emit("scalar", lambda s, par_=spar, pair_=pair, eo=eoff: s.activation(
                            etb_t[:, eo + pair_ * 1024:eo + (pair_ + 1) * 1024], ps_s[par_][:, :],
                            AF.Exp, scale=0.125),
                            waits=we, incs=[(sSC, 1)])
                        m_exp[((qb, h), pair)] = sSC.n
                        pair_exp[gp] = sSC.n
                    if pend is not None:
                        sigma_recip(pend)
                    if pend2 is not None:
                        sigma_rest(pend2)
                        if pend2[0][1] == NH - 1:
                            emit_outproj(pend2[0][0])
                    pend2 = pend
                    pend = ((qb, h), take_slot())
                    dc(3)
            for pair in range(NKC // 2):
                emit_ot_pair(pend, pair)
            sigma_recip(pend)
            sigma_rest(pend2)
            sigma_rest(pend)
            emit_outproj(NQB - 1)
            dc(100000)
            mark["wslot"] = sT.n
        ctx.close()

    NW1SLOT = 4

    def phaseB(b, n_scr):
        barrier(extra=[(sCAST[0], sCAST[0].n), (sCAST[1], sCAST[1].n),
                       (sCSTO[0], sCSTO[0].n), (sCSTO[1], sCSTO[1].n)])
        ctx = contextlib.ExitStack()
        w1r = ctx.enter_context(nc.sbuf_tensor(f"w1r{b}", [128, NW1SLOT, CC * 512], BF16))
        wor = ctx.enter_context(nc.sbuf_tensor(f"wor{b}", [128, 2, 4 * D], BF16))
        interT = ctx.enter_context(nc.sbuf_tensor(f"interT{b}", [128, 32, 512], BF16))
        gbuf = ctx.enter_context(nc.sbuf_tensor(f"gbuf{b}", [128, 2, NE, 512], BF16))
        selb = ctx.enter_context(nc.sbuf_tensor(f"sel{b}", [128, 2, 512], BF16))
        mbc = ctx.enter_context(nc.sbuf_tensor(f"mbc{b}", [128, NE, 512], BF16))
        diag_bf = ctx.enter_context(nc.sbuf_tensor(f"diag{b}", [128, 2, 128], BF16))
        rows_slot = ctx.enter_context(nc.sbuf_tensor(f"rowsl{b}", [1, 512], F32))
        gam_bc = ctx.enter_context(nc.sbuf_tensor(f"gam_bc{b}", [128, D], F32))
        bout_bc = ctx.enter_context(nc.sbuf_tensor(f"bout_bc{b}", [128, D], F32))
        bo_bc = ctx.enter_context(nc.sbuf_tensor(f"bo_bc{b}", [128, D], F32))
        ps8 = ctx.enter_context(nc.psum_tensor(f"ps8_{b}", [128, 4096], F32))

        m_gelu = {}
        m_sel = {}
        m_fin1 = {}
        m_fin = {}
        m_yis = {}
        m_w1slot = {}
        m_w1ld = {}
        m_wosl = {}
        m_wold = {}
        m_mbc_ev = 0
        m_scfin = [0]
        m_wb = mark.setdefault("wb", {})
        m_y_done_prev = 0
        m_xfree = {}
        m_fin_prev = [0]
        m_yeread = [(sV, 0), (sV, 0)]
        fin_pend = [None]

        def drain_fin(k):
            gen = fin_pend[0]
            if gen is None:
                return
            for _ in range(k):
                if next(gen, "DONE") == "DONE":
                    fin_pend[0] = None
                    return

        def fin_emit(tb, m_y_done):
            for tt in (0, 3, 1, 2):
                t_loc = tb * 4 + tt
                g = b * NT + t_loc
                p = tt % 2
                E.emit("sync", lambda s, g_=g, p_=p: s.dma_start(
                    ar_xbuf[p_], x_d[g_ * 128:(g_ + 1) * 128, :]),
                    waits=[(sV, m_xfree.get(p, 0))],
                    incs=[(sX[p], 16)])
                mx = sX[p].n
                so = 8 * p
                if tt in (0, 3):
                    ysrc = ar_out[0 if tt == 0 else 1]
                    w1_ = [m_fin1[(tb, tt)], (sV, m_fin_prev[0])]
                else:
                    ysrc = ps8[:, tt * 1024:(tt + 1) * 1024]
                    w1_ = [(sT, m_y_done), (sV, m_fin_prev[0])]
                E.emit("vector", lambda v, ys=ysrc: v.tensor_tensor(
                    ar_fb, ys, bo_bc[:, :], op=ALU.add),
                    waits=w1_, incs=[(sV, 1)])
                if tt in (1, 2):
                    m_fin1[(tb, tt)] = (sV, sV.n)
                else:
                    m_yeread[0 if tt == 0 else 1] = (sV, sV.n)
                yield
                E.emit("vector", lambda v, p_=p, g_=g: v.tensor_scalar(
                    ar_fa, ar_xbuf[p_], murstd[:, g_:g_ + 1],
                    murstd[:, nb * NT + g_:nb * NT + g_ + 1],
                    op0=ALU.subtract, op1=ALU.mult),
                    waits=[(sX[p], mx)], incs=[(sV, 1)])
                mv = sV.n
                yield
                E.emit("vector", lambda v: v.tensor_tensor(ar_fa, ar_fb, ar_fa, op=ALU.add),
                       waits=[(sV, mv)], incs=[(sV, 1)])
                mv = sV.n
                yield
                E.emit("vector", lambda v, so_=so: v.tensor_reduce(
                    stats[:, so_ + 0:so_ + 1], ar_fa, axis=AX.X, op=ALU.add),
                    waits=[(sV, mv), (sSC, m_scfin[0])], incs=[(sV, 1)])
                mv1 = sV.n
                yield
                E.emit("vector", lambda v: v.tensor_tensor(ar_fb, ar_fa, ar_fa, op=ALU.mult),
                       waits=[(sV, mv)], incs=[(sV, 1)])
                mv2 = sV.n
                yield
                E.emit("vector", lambda v, so_=so: v.tensor_reduce(
                    stats[:, so_ + 1:so_ + 2], ar_fb, axis=AX.X, op=ALU.add),
                    waits=[(sV, mv2)], incs=[(sV, 1)])
                mv3 = sV.n
                yield
                E.emit("vector", lambda v, so_=so: v.tensor_scalar(
                    stats[:, so_ + 2:so_ + 3], stats[:, so_ + 0:so_ + 1], 1.0 / D, None,
                    op0=ALU.mult),
                    waits=[(sV, mv1)], incs=[(sV, 1)])
                mv4 = sV.n
                E.emit("vector", lambda v, so_=so: v.tensor_scalar(
                    stats[:, so_ + 3:so_ + 4], stats[:, so_ + 1:so_ + 2], 1.0 / D, None,
                    op0=ALU.mult),
                    waits=[(sV, mv3)], incs=[(sV, 1)])
                mv5 = sV.n
                yield
                E.emit("vector", lambda v, so_=so: v.tensor_tensor(
                    stats[:, so_ + 4:so_ + 5], stats[:, so_ + 2:so_ + 3],
                    stats[:, so_ + 2:so_ + 3], op=ALU.mult),
                    waits=[(sV, mv4)], incs=[(sV, 1)])
                mv6 = sV.n
                E.emit("vector", lambda v, so_=so: v.tensor_tensor(
                    stats[:, so_ + 5:so_ + 6], stats[:, so_ + 3:so_ + 4],
                    stats[:, so_ + 4:so_ + 5], op=ALU.subtract),
                    waits=[(sV, mv5), (sV, mv6)], incs=[(sV, 1)])
                mv7 = sV.n
                yield
                E.emit("scalar", lambda s, so_=so: s.activation(
                    stats[:, so_ + 6:so_ + 7], stats[:, so_ + 5:so_ + 6], AF.Ln, bias=cst[:, 1:2]),
                    waits=[(sV, mv7)], incs=[(sSC, 1)])
                msc = sSC.n
                E.emit("scalar", lambda s, so_=so: s.activation(
                    stats[:, so_ + 7:so_ + 8], stats[:, so_ + 6:so_ + 7], AF.Exp, scale=-0.5),
                    waits=[(sSC, msc)], incs=[(sSC, 1)])
                m_scfin[0] = sSC.n
                yield
                E.emit("vector", lambda v, so_=so: v.tensor_scalar(
                    ar_fb, ar_fa, stats[:, so_ + 2:so_ + 3], stats[:, so_ + 7:so_ + 8],
                    op0=ALU.subtract, op1=ALU.mult),
                    waits=[(sSC, m_scfin[0])], incs=[(sV, 1)])
                mv = sV.n
                yield
                E.emit("vector", lambda v: v.tensor_tensor(ar_fa, ar_fb, gam_bc[:, :], op=ALU.mult),
                       waits=[(sV, mv)], incs=[(sV, 1)])
                mv = sV.n
                yield
                E.emit("vector", lambda v: v.tensor_tensor(ar_fb, ar_fa, bout_bc[:, :], op=ALU.add),
                       waits=[(sV, mv)], incs=[(sV, 1)])
                mv = sV.n
                yield
                E.emit("vector", lambda v, tl=t_loc: v.tensor_tensor(
                    ar_fa, ar_fb, attn_sb[:, tl, :], op=ALU.add),
                    waits=[(sV, mv)], incs=[(sV, 1)])
                mv = sV.n
                yield
                E.emit("vector", lambda v, p_=p: v.tensor_tensor(
                    ar_out[p_], ar_fa, ar_xbuf[p_], op=ALU.add),
                    waits=[(sV, mv), (sOU[p], sOU[p].n)],
                    incs=[(sV, 1)])
                m_fin[(tb, tt)] = sV.n
                m_xfree[p] = sV.n
                m_fin_prev[0] = sV.n
                E.emit("sync", lambda s, g_=g, p_=p: s.dma_start(
                    out_d[g_ * 128:(g_ + 1) * 128, :], ar_out[p_]),
                    waits=[(sV, m_fin[(tb, tt)])], incs=[(sOU[p], 16)])
                yield

        m_scr = mark["scr_pieces"]

        def emit_w1_load(tb, ii):
            piece = (ii % NE) * NIS + (ii // NE)
            slot = ii % NW1SLOT
            w = [(sT, m_w1slot.get(slot, 0))] + list(m_scr[piece])
            E.emit("sync", lambda s, p_=piece, sl_=slot: s.dma_start(
                w1r[:, sl_, :], scrW1[p_, :, :]),
                waits=w, incs=[(sW1[slot], 16)])
            m_w1ld[(tb, ii)] = (sW1[slot], sW1[slot].n)

        def emit_wo_load(tb, is_):
            slot = is_ % 2
            w = [(sT, m_wosl.get(slot, 0))]
            E.emit("gpsimd", lambda g, p_=is_, sl_=slot: g.dma_start(
                wor[:, sl_, :], wo_d[p_, :, :]),
                waits=w, incs=[(sWog[slot], 16)])
            m_wold[(tb, is_)] = (sWog[slot], sWog[slot].n)

        for tb in range(NTB):
            for ii in range(NW1SLOT):
                emit_w1_load(tb, ii)
            for is_ in range(2):
                emit_wo_load(tb, is_)
            if tb == 0:
                mrl = 0
                for dst, srcs in [(gam_bc, [0]), (bout_bc, [1, 2]), (bo_bc, [3])]:
                    for half in range(2):
                        for i, r in enumerate(srcs):
                            E.emit("sync", lambda s, r_=r, h=half: s.dma_start(
                                rows_slot[0:1, :],
                                rows_d[r_:r_ + 1, h * 512:(h + 1) * 512]),
                                waits=[(sT, mrl)], incs=[(sROW, 16)])
                            mrow = sROW.n
                            E.emit("tensor", lambda t, f_=(i == 0), l_=(i == len(srcs) - 1):
                                   t.matmul(ps8[:, 3584:4096], ones_f[0:1, :],
                                            rows_slot[0:1, :],
                                            start=f_, stop=l_),
                                   waits=[(sROW, mrow), (sV, mark["ones_v"]),
                                          (sV, m_mbc_ev)],
                                   incs=[(sT, 1)])
                            mrl = sT.n
                        mtb = sT.n
                        E.emit("vector", lambda v, d_=dst, h=half: v.tensor_copy(
                            d_[:, h * 512:(h + 1) * 512], ps8[:, 3584:4096]),
                            waits=[(sT, mtb)], incs=[(sV, 1)])
                        m_mbc_ev = sV.n
            m_diag = [0, 0]
            for e in range(NE):
                for tl in range(4):
                    gt_ = b * NT + tb * 4 + tl
                    dp = tl % 2
                    E.emit("vector", lambda v, e_=e, g2=gt_, dp_=dp: v.tensor_scalar(
                        diag_bf[:, dp_, :], eye_bf[:, :],
                        masks_all[:, NE * g2 + e_:NE * g2 + e_ + 1], None, op0=ALU.mult),
                        waits=[(sV, mark["A_v"] if (e == 0 and tl == 0) else 0),
                               (sT, m_diag[dp]),
                               m_fin1.get((tb - 1, 3), (sV, 0))],
                        incs=[(sV, 1)])
                    mdg = sV.n
                    E.emit("tensor", lambda t_, tl_=tl, dp_=dp: t_.matmul(
                        ps8[:, 3584 + tl_ * 128:3584 + (tl_ + 1) * 128],
                        onesq_bf[:, :], diag_bf[:, dp_, :], start=True, stop=True),
                        waits=[(sV, mdg), (sV, m_mbc_ev if tl < 2 else 0)],
                        incs=[(sT, 1)])
                    m_diag[dp] = sT.n
                E.emit("vector", lambda v, e_=e: v.tensor_copy(mbc[:, e_, :], ps8[:, 3584:4096]),
                       waits=[(sT, m_diag[0]), (sT, m_diag[1])], incs=[(sV, 1)])
                m_mbc_ev = sV.n
            for is_ in range(NIS):
                for ic4 in range(4):
                    icg = is_ * 4 + ic4
                    for e in range(NE):
                        grp = icg * NE + e
                        gpar = grp % 2
                        ii = is_ * NE + e
                        slot = ii % NW1SLOT
                        pg = grp - 2
                        m_prev_half = (m_gelu.get((tb, pg // NE, pg % NE), 0)
                                       if pg >= 0 else 0)
                        for cc in range(CC):
                            w = []
                            if cc == 0:
                                w = [m_w1ld[(tb, ii)],
                                     (sSC, m_prev_half),
                                     m_fin1.get((tb - 1, 0), (sV, 0))]
                            E.emit("tensor", lambda t_, sl_=slot, cc_=cc, ic_=ic4, gp_=gpar, tb_=tb:
                                   t_.matmul(ps8[:, gp_ * 512:(gp_ + 1) * 512],
                                             w1r[:, sl_, cc_ * 512 + ic_ * 128:cc_ * 512 + (ic_ + 1) * 128],
                                             xnT[:, cc_, 8 + tb_ * 512:8 + (tb_ + 1) * 512],
                                             start=(cc_ == 0), stop=(cc_ == CC - 1)),
                                   waits=w, incs=[(sT, 1)] if cc == CC - 1 else ())
                        m_hmm = sT.n
                        m_w1slot[slot] = m_hmm
                        if ic4 == 3 and ii + NW1SLOT < NIS * NE:
                            emit_w1_load(tb, ii + NW1SLOT)
                        E.emit("scalar", lambda s, e_=e, icg_=icg, gp_=gpar: s.activation(
                            gbuf[:, icg_ % 2, e_, :], ps8[:, gp_ * 512:(gp_ + 1) * 512], AF.Gelu,
                            bias=b1_sb[:, e_ * 32 + icg_:e_ * 32 + icg_ + 1]),
                            waits=[(sT, m_hmm),
                                   (sV, m_sel.get((tb, icg - 2), m_sel.get((tb - 1, 30 + (icg % 2)), 0)))],
                            incs=[(sSC, 1)])
                        m_gelu[(tb, icg, e)] = sSC.n
                    mg = m_gelu[(tb, icg, NE - 1)]
                    gp2 = icg % 2
                    E.emit("vector", lambda v, g2=gp2: v.tensor_tensor(
                        selb[:, 0, :], gbuf[:, g2, 0, :], mbc[:, 0, :], op=ALU.mult),
                        waits=[(sSC, mg), (sV, m_mbc_ev),
                               (sV, m_sel.get((tb, icg - 1), m_sel.get((tb - 1, 31), 0)))])
                    E.emit("vector", lambda v, g2=gp2: v.tensor_tensor(
                        selb[:, 1, :], gbuf[:, g2, 1, :], mbc[:, 1, :], op=ALU.mult),
                        incs=[(sV, 1)])
                    ms1_ = sV.n
                    E.emit("vector", lambda v: v.tensor_tensor(
                        selb[:, 0, :], selb[:, 0, :], selb[:, 1, :], op=ALU.add),
                        waits=[(sV, ms1_)], incs=[(sV, 1)])
                    ms2_ = sV.n
                    E.emit("vector", lambda v, g2=gp2: v.tensor_tensor(
                        selb[:, 1, :], gbuf[:, g2, 2, :], mbc[:, 2, :], op=ALU.mult),
                        waits=[(sV, ms2_)], incs=[(sV, 1)])
                    ms3_ = sV.n
                    E.emit("vector", lambda v: v.tensor_tensor(
                        selb[:, 0, :], selb[:, 0, :], selb[:, 1, :], op=ALU.add),
                        waits=[(sV, ms3_)], incs=[(sV, 1)])
                    ms35_ = sV.n
                    E.emit("vector", lambda v, g2=gp2: v.tensor_tensor(
                        selb[:, 1, :], gbuf[:, g2, 3, :], mbc[:, 3, :], op=ALU.mult),
                        waits=[(sV, ms35_)], incs=[(sV, 1)])
                    ms4_ = sV.n
                    E.emit("vector", lambda v, icg_=icg: v.tensor_tensor(
                        interT[:, icg_, :], selb[:, 0, :], selb[:, 1, :], op=ALU.add),
                        waits=[(sV, ms4_), (sT, m_y_done_prev)],
                        incs=[(sV, 1)])
                    m_sel[(tb, icg)] = sV.n
                    drain_fin(2)
            drain_fin(10000)
            m_gelu_all = m_gelu[(tb, 31, NE - 1)]
            for is_ in range(NIS):
                for tt in range(4):
                    for ic4 in range(4):
                        for oh in range(2):
                            w = []
                            if ic4 == 0 and oh == 0:
                                w = [(sV, m_sel[(tb, is_ * 4 + 3)]),
                                     m_wold[(tb, is_)]]
                                if is_ == 0:
                                    w.append(m_fin1.get((tb - 1, tt), (sV, 0)))
                                    w.append((sSC, m_gelu_all))
                                    if tt == 3:
                                        w.append((sV, m_mbc_ev))
                            E.emit("tensor", lambda t_, is2=is_, tt_=tt, ic_=ic4, oh_=oh:
                                   t_.matmul(
                                       ps8[:, tt_ * 1024 + oh_ * 512:tt_ * 1024 + (oh_ + 1) * 512],
                                       interT[:, is2 * 4 + ic_, tt_ * 128:(tt_ + 1) * 128],
                                       wor[:, is2 % 2, ic_ * D + oh_ * 512:ic_ * D + (oh_ + 1) * 512],
                                       start=(is2 == 0 and ic_ == 0),
                                       stop=(is2 == NIS - 1 and ic_ == 3)),
                                   waits=w,
                                   incs=[(sT, 1)] if (tt == 3 and ic4 == 3 and oh == 1) else ())
                m_yis[(tb, is_)] = sT.n
                m_wosl[is_ % 2] = sT.n
                if is_ + 2 < NIS:
                    emit_wo_load(tb, is_ + 2)
            m_y_done_prev = sT.n
            m_y_done = sT.n
            E.emit("scalar", lambda s: s.copy(ar_out[0], ps8[:, 0:1024]),
                   waits=[(sT, m_y_done), m_yeread[0], (sOU[0], sOU[0].n)],
                   incs=[(sSC, 1)])
            m_fin1[(tb, 0)] = (sSC, sSC.n)
            E.emit("scalar", lambda s: s.copy(ar_out[1], ps8[:, 3072:4096]),
                   waits=[m_yeread[1], (sOU[1], sOU[1].n)], incs=[(sSC, 1)])
            m_fin1[(tb, 3)] = (sSC, sSC.n)
            fin_pend[0] = fin_emit(tb, m_y_done)
            if tb == NTB - 1:
                drain_fin(10000)
        mark["xnT_free"] = sT.n
        mark["masksT_free"] = sT.n
        mark["attn_free"] = sV.n
        mark["arena_fin"] = sV.n
        ctx.close()


    for b in range(nb):
        phaseA(b)
        phaseC(b, n_scr_total)
        if b == 0:
            cast_ctx.close()
        phaseB(b, n_scr_total)

    E.emit("sync", None, waits=[(sOU[0], sOU[0].n), (sOU[1], sOU[1].n)])

    with nc.Block() as blk:
        blk.sync(lambda e: E.run("sync", e))
        blk.scalar(lambda e: E.run("scalar", e))
        blk.vector(lambda e: E.run("vector", e))
        blk.tensor(lambda e: E.run("tensor", e))
        blk.gpsimd(lambda e: E.run("gpsimd", e))

    st.close()
    nc.finalize()
    return nc


def kernel(**inputs):
    x = np.asarray(inputs["L0_attn_x"], dtype=np.float32)
    seq = int(x.shape[1])
    nb = x.shape[0] // N_CORES
    packed = pack_inputs(inputs)
    nc = build_nc(seq=seq, nb=nb)
    in_maps = []
    for c in range(N_CORES):
        m = dict(packed)
        m["x"] = np.ascontiguousarray(x[c * nb:(c + 1) * nb].reshape(nb * seq, D))
        in_maps.append(m)
    res = run_bass_kernel_spmd(nc, in_maps, core_ids=list(range(N_CORES)))
    outs = [np.asarray(res.results[c]["out"]).reshape(nb, seq, D) for c in range(N_CORES)]
    return np.concatenate(outs, axis=0)


if __name__ == "__main__":
    build_nc()
    print("build ok")
